# version 22
# baseline (speedup 1.0000x reference)
import numpy as np
from contextlib import ExitStack

import concourse.bass as bass
import concourse.mybir as mybir
from concourse.bass_utils import run_bass_kernel_spmd

F32 = mybir.dt.float32
BF16 = mybir.dt.bfloat16
AF = mybir.ActivationFunctionType
ALU = mybir.AluOpType
AX = mybir.AxisListType

NCORES = 8
B, S, D = 16, 2048, 1024
TOK = B * S
TPC = TOK // NCORES
MT = 1024
NMACRO = TPC // MT
NCH = MT // 128
DC = D // 128
M = 1408
MC = M // 128
E = 8
HD = 64


def build_program(debug=False):
    nc = bass.Bass("TRN2", target_bir_lowering=False,
                   detect_race_conditions=False)
    x_d = nc.dram_tensor("x", [TPC, D], F32, kind="ExternalInput").ap()
    wkt_d = nc.dram_tensor("wkt", [D, D], F32, kind="ExternalInput").ap()
    w1t_d = nc.dram_tensor("w1t", [D, HD], F32, kind="ExternalInput").ap()
    w2t_d = nc.dram_tensor("w2t", [HD, D], F32, kind="ExternalInput").ap()
    wg8_d = nc.dram_tensor("wgate", [D, E], F32, kind="ExternalInput").ap()
    wg_d = nc.dram_tensor("wg", [E, D, M], F32, kind="ExternalInput").ap()
    wu_d = nc.dram_tensor("wu", [E, D, M], F32, kind="ExternalInput").ap()
    wd_d = nc.dram_tensor("wd", [E, M, D], F32, kind="ExternalInput").ap()
    y_d = nc.dram_tensor("y", [TPC, D], F32, kind="ExternalOutput").ap()
    dbg_d = {}
    if debug:
        for nm, shape in [("d_x1", [128, NCH * D]), ("d_lnf", [128, DC * MT]),
                          ("d_kfm", [128, DC * 512]), ("d_a", [128, MT]),
                          ("d_scs", [128, NCH * E]), ("d_gat", [128, NCH * E]),
                          ("d_lnf2", [128, DC * MT]), ("d_ltm", [128, NCH * D]), ("d_x0b", [128, NCH * D]), ("d_st", [128, 48])]:
            dbg_d[nm] = nc.dram_tensor(nm, shape, F32, kind="ExternalOutput").ap()

    import ml_dtypes
    ident_d = nc.inline_tensor(np.eye(128, dtype=ml_dtypes.bfloat16), name="ident")
    wgbf_d = nc.dram_tensor("wgbf", [E, 128, DC * M], BF16, kind="Internal").ap()
    wubf_d = nc.dram_tensor("wubf", [E, 128, DC * M], BF16, kind="Internal").ap()
    wdbf_d = nc.dram_tensor("wdbf", [E, 128, MC * D], BF16, kind="Internal").ap()

    with ExitStack() as ctx:
        SB = lambda name, shape, dt: ctx.enter_context(nc.sbuf_tensor(name, shape, dt))
        PS = lambda name, shape, dt: ctx.enter_context(nc.psum_tensor(name, shape, dt))

        acc = SB("acc", [128, NCH * D], F32)
        x0buf = SB("x0buf", [128, NCH * D], BF16)
        ln_tm = SB("ln_tm", [128, NCH * D], BF16)
        lnf = SB("lnf", [128, DC * MT], BF16)
        kfm = SB("kfm", [128, DC * 512], BF16)
        a_sb = SB("a_sb", [128, MT], BF16)
        hid = SB("hid", [128, MC * MT], BF16)
        wgs = SB("wgs", [128, DC * M], BF16)
        wus = SB("wus", [128, DC * M], BF16)
        wds = SB("wds", [128, MC * D], BF16)
        wkt = SB("wkt_sb", [128, DC * D], BF16)
        w1t = SB("w1t_sb", [128, DC * HD], BF16)
        w2t = SB("w2t_sb", [128, D], BF16)
        wg8 = SB("wg8_sb", [128, DC * E], BF16)
        ident = SB("ident_sb", [128, 128], BF16)
        sqscr = SB("sqscr", [128, D], F32)
        silu0 = SB("silu0", [128, 512], BF16)
        silu1 = SB("silu1", [128, 512], BF16)
        sum8 = SB("sum8", [128, NCH], F32)
        sq8 = SB("sq8", [128, NCH], F32)
        mean8 = SB("mean8", [128, NCH], F32)
        exsq8 = SB("exsq8", [128, NCH], F32)
        msq8 = SB("msq8", [128, NCH], F32)
        var8 = SB("var8", [128, NCH], F32)
        lnv8 = SB("lnv8", [128, NCH], F32)
        inv8 = SB("inv8", [128, NCH], F32)
        scs = SB("scs", [128, NCH * E], F32)
        mx8 = SB("mx8", [128, NCH * E], F32)
        nm18 = SB("nm18", [128, NCH], F32)
        exall = SB("exall", [128, NCH * E], F32)
        den8 = SB("den8", [128, NCH], F32)
        den8p = SB("den8p", [128, NCH], F32)
        rden8 = SB("rden8", [128, NCH], F32)
        mskall = SB("mskall", [128, NCH * E], F32)
        gat = SB("gat", [128, NCH * E], F32)
        nmi8 = SB("nmi8", [128, NCH], F32)
        bias2 = SB("bias2", [128, NCH], F32)
        invD = SB("invD", [128, 1], F32)
        epsc = SB("epsc", [128, 1], F32)
        nhc = SB("nhc", [128, 1], F32)
        onec = SB("onec", [128, 1], F32)
        negone = SB("negone", [128, 1], F32)

        pbt = PS("pbt", [128, 1024], BF16)
        pk = [PS("pk0", [128, 512], F32), PS("pk1", [128, 512], F32)]
        pg = [PS("pg0", [128, 512], F32), PS("pg1", [128, 512], F32)]
        pu = [PS("pu0", [128, 512], F32), PS("pu1", [128, 512], F32)]
        pbt2 = PS("pbt2", [128, 1024], BF16)

        sems = {}
        for name in ("LD", "XB", "WA", "WG", "WU", "WD", "T", "V", "S", "ST", "DB"):
            sems[name] = ctx.enter_context(nc.semaphore(name))

        steps = []
        cnt = {k: 0 for k in sems}

        def emit(eng, fn, waits=(), inc=None, inc_n=1, fwaits=()):
            steps.append({"eng": eng, "waits": [w for w in waits if w[1] > 0],
                          "fwaits": [w for w in fwaits if w[1] > 0],
                          "fn": fn, "inc": inc, "inc_n": inc_n})
            if inc is not None:
                cnt[inc] += inc_n
            return cnt[inc] if inc is not None else None

        emit("V", lambda e: e.memset(invD[:], 1.0 / D))
        emit("V", lambda e: e.memset(epsc[:], 1e-5))
        emit("V", lambda e: e.memset(nhc[:], -0.5))
        emit("V", lambda e: e.memset(negone[:], -1.0))
        emit("V", lambda e: e.memset(onec[:], 1.0), inc="V")
        v_consts = cnt["V"]

        emit("SY", lambda e: e.dma_start(out=ident[:], in_=ident_d.ap()), inc="LD", inc_n=16)
        ld_ident = cnt["LD"]

        st_all_prev = 0
        ld_x_next = []
        for j in range(NCH):
            emit("G", lambda e, j=j: e.dma_start(
                out=x0buf[:, j * D:(j + 1) * D],
                in_=x_d[j * 128:(j + 1) * 128, :]), inc="XB", inc_n=16)
        ld_x_next = cnt["XB"]

        for hc in range(DC):
            emit("G", lambda e, hc=hc: e.dma_start(
                out=wkt[:, hc * D:(hc + 1) * D], in_=wkt_d[hc * 128:(hc + 1) * 128, :]),
                inc="WA", inc_n=16)
        for kc in range(DC):
            emit("G", lambda e, kc=kc: e.dma_start(
                out=w1t[:, kc * HD:(kc + 1) * HD], in_=w1t_d[kc * 128:(kc + 1) * 128, :]),
                inc="WA", inc_n=16)
        emit("G", lambda e: e.dma_start(out=w2t[0:HD, :], in_=w2t_d), inc="WA", inc_n=16)
        for dc in range(DC):
            emit("G", lambda e, dc=dc: e.dma_start(
                out=wg8[:, dc * E:(dc + 1) * E], in_=wg8_d[dc * 128:(dc + 1) * 128, :]),
                inc="WA", inc_n=16)
        wt_attn = cnt["WA"]

        wt_wrote = {"g": {}, "u": {}, "d": {}}

        def load_expert(e_idx, wait_t_gu, wait_t_d, cached):
            if not cached:
                wg0, wu0, wd0 = cnt["WG"], cnt["WU"], cnt["WD"]
                for dc in range(DC):
                    emit("G", lambda e, dc=dc, ei=e_idx: e.dma_start(
                        out=wgs[:, dc * M:(dc + 1) * M],
                        in_=wg_d[ei, dc * 128:(dc + 1) * 128, :]),
                        waits=[("T", wait_t_gu), ("WG", wg0)], inc="WG", inc_n=16)
                wt_lg = cnt["WG"]
                for dc in range(DC):
                    emit("G", lambda e, dc=dc, ei=e_idx: e.dma_start(
                        out=wus[:, dc * M:(dc + 1) * M],
                        in_=wu_d[ei, dc * 128:(dc + 1) * 128, :]),
                        waits=[("T", wait_t_gu), ("WU", wu0)], inc="WU", inc_n=16)
                wt_lu = cnt["WU"]
                emit("G", lambda e, ei=e_idx: e.dma_start(
                    out=wgbf_d[ei], in_=wgs[:]),
                    waits=[("WG", wt_lg)], inc="WG", inc_n=16)
                wt_wrote["g"][e_idx] = cnt["WG"]
                emit("G", lambda e, ei=e_idx: e.dma_start(
                    out=wubf_d[ei], in_=wus[:]),
                    waits=[("WU", wt_lu)], inc="WU", inc_n=16)
                wt_wrote["u"][e_idx] = cnt["WU"]
                gu_waits = [("WG", wt_lg), ("WU", wt_lu)]
                for mc in range(MC):
                    emit("G", lambda e, mc=mc, ei=e_idx: e.dma_start(
                        out=wds[:, mc * D:(mc + 1) * D],
                        in_=wd_d[ei, mc * 128:(mc + 1) * 128, :]),
                        waits=[("T", wait_t_d), ("WD", wd0)], inc="WD", inc_n=16)
                wt_ld = cnt["WD"]
                emit("G", lambda e, ei=e_idx: e.dma_start(
                    out=wdbf_d[ei], in_=wds[:]),
                    waits=[("WD", wt_ld)], inc="WD", inc_n=16)
                wt_wrote["d"][e_idx] = cnt["WD"]
                d_waits = [("WD", wt_ld)]
            else:
                emit("G", lambda e, ei=e_idx: e.dma_start(
                    out=wgs[:], in_=wgbf_d[ei]),
                    waits=[("T", wait_t_gu), ("WG", cnt["WG"])],
                    inc="WG", inc_n=16)
                wg_c = cnt["WG"]
                emit("G", lambda e, ei=e_idx: e.dma_start(
                    out=wus[:], in_=wubf_d[ei]),
                    waits=[("T", wait_t_gu), ("WU", cnt["WU"])],
                    inc="WU", inc_n=16)
                gu_waits = [("WG", wg_c), ("WU", cnt["WU"])]
                emit("G", lambda e, ei=e_idx: e.dma_start(
                    out=wds[:], in_=wdbf_d[ei]),
                    waits=[("T", wait_t_d), ("WD", cnt["WD"])],
                    inc="WD", inc_n=16)
                d_waits = [("WD", cnt["WD"])]
            return gu_waits, d_waits

        wt_expert = load_expert(0, 0, 0, cached=False)

        NLEXP_SET, GELU_SET, SILU_SET = 6, 10, 18

        def preload_set(sid):
            pass

        v_ln1t_done_prev = 0

        sqscr_last = [0]

        def emit_stats(src, j, w_s, w_v):
            emit("S", lambda e, j=j: e.activation(
                sqscr[:], src[:, j * D:(j + 1) * D], AF.Square,
                accum_out=sq8[:, j:j + 1]), waits=w_s,
                fwaits=[("S", sqscr_last[0])], inc="S")
            sqscr_last[0] = cnt["S"]
            s_m = cnt["S"]
            emit("V", lambda e, j=j: e.reduce_sum(
                sum8[:, j:j + 1], src[:, j * D:(j + 1) * D], axis=AX.X),
                waits=w_v + [("S", s_m)], inc="V")

        def ln_chain(stats_done, c0=0, c1=NCH):
            v_sums, s_sqs = stats_done
            cs = slice(c0, c1)
            emit("S", lambda e: e.activation(mean8[:, cs], sum8[:, cs],
                                             AF.Identity, scale=invD[:, 0:1]),
                 waits=[("V", v_sums)], inc="S")
            emit("S", lambda e: e.activation(exsq8[:, cs], sq8[:, cs],
                                             AF.Identity, scale=invD[:, 0:1]),
                 inc="S")
            emit("S", lambda e: e.activation(msq8[:, cs], mean8[:, cs],
                                             AF.Square), inc="S")
            s_stats = cnt["S"]
            emit("V", lambda e: e.tensor_sub(var8[:, cs], exsq8[:, cs],
                                             msq8[:, cs]),
                 waits=[("S", s_stats)], inc="V")
            v_var = cnt["V"]
            emit("S", lambda e: e.activation(lnv8[:, cs], var8[:, cs], AF.Ln,
                                             bias=epsc[:, 0:1]),
                 waits=[("V", v_var)], inc="S")
            emit("S", lambda e: e.activation(inv8[:, cs], lnv8[:, cs], AF.Exp,
                                             scale=nhc[:, 0:1]), inc="S")
            s_inv = cnt["S"]
            emit("V", lambda e: e.tensor_mul(nmi8[:, cs], mean8[:, cs],
                                             inv8[:, cs]),
                 waits=[("S", s_inv)])
            emit("V", lambda e: e.tensor_scalar_mul(bias2[:, cs], nmi8[:, cs],
                                                    negone[:, 0:1]), inc="V")
            return s_inv, cnt["V"]

        def ln_normalize(src, s_inv, v_b2, t_lntm_free, chunks=range(NCH)):
            norm_marker = {}
            for j in chunks:
                if j % 2 == 0 or j == 7:
                    emit("V", lambda e, j=j: e.tensor_scalar(
                        ln_tm[:, j * D:(j + 1) * D], src[:, j * D:(j + 1) * D],
                        mean8[:, j:j + 1], inv8[:, j:j + 1],
                        op0=ALU.subtract, op1=ALU.mult),
                        waits=[("S", s_inv), ("T", t_lntm_free)], inc="V")
                    norm_marker[j] = ("V", cnt["V"])
                else:
                    emit("S", lambda e, j=j: e.activation(
                        ln_tm[:, j * D:(j + 1) * D], src[:, j * D:(j + 1) * D],
                        AF.Identity, scale=inv8[:, j:j + 1],
                        bias=bias2[:, j:j + 1]),
                        waits=[("V", v_b2), ("T", t_lntm_free)], inc="S")
                    norm_marker[j] = ("S", cnt["S"])
            return norm_marker

        def ln_chain_norm(src, stats_done, t_lntm_free):
            s_inv, v_b2 = ln_chain(stats_done)
            return ln_normalize(src, s_inv, v_b2, t_lntm_free)

        pbt_state = {}

        def ln_transpose(norm_marker, chunks=range(NCH)):
            lnf_v = lnf[:].rearrange("p (c t) -> p c t", t=MT)
            banks = (pbt, pbt2)
            v_evac_last = s_evac_last = None
            for j in chunks:
                bank = banks[j % 2]
                bank_v = bank[:].rearrange("p (c t) -> p c t", t=128)
                for c in range(DC):
                    w = []
                    if c == 0:
                        w = [norm_marker[j], ("LD", ld_ident)]
                        prev = pbt_state.get(j % 2)
                        if prev:
                            w.append(prev)
                    emit("T", lambda e, c=c, j=j, bank=bank: e.transpose(
                        bank[:, c * 128:(c + 1) * 128],
                        ln_tm[:, j * D + c * 128: j * D + (c + 1) * 128],
                        ident[:]),
                        waits=w, inc="T" if c == DC - 1 else None)
                t_j = cnt["T"]
                emit("V", lambda e, j=j, bank_v=bank_v: e.tensor_copy(
                    lnf_v[:, :, j * 128:(j + 1) * 128], bank_v[:]),
                    waits=[("T", t_j)], inc="V")
                pbt_state[j % 2] = ("V", cnt["V"])
                v_evac_last = cnt["V"]
                s_evac_last = cnt["S"]
            return v_evac_last, s_evac_last, cnt["T"]

        front_marker = None

        for m in range(NMACRO):
            tok0 = m * MT
            ld_x = ld_x_next

            if front_marker is None:
                preload_set(NLEXP_SET)
                for j in range(NCH):
                    emit_stats(x0buf, j, [("XB", ld_x)], [("XB", ld_x)])
                front_marker = ln_chain_norm(x0buf, (cnt["V"], cnt["S"]),
                                             v_ln1t_done_prev)
            v_lnf1, s_lnf1, t_ln1t = ln_transpose(front_marker)
            front_marker = None
            if debug and m == 1:
                emit("G", lambda e: e.dma_start(out=dbg_d["d_lnf"], in_=lnf[:]),
                     waits=[("V", v_lnf1), ("S", s_lnf1)], inc="DB", inc_n=16)
                emit("G", lambda e: e.dma_start(out=dbg_d["d_ltm"], in_=ln_tm[:]),
                     waits=[("V", v_lnf1), ("S", s_lnf1)], inc="DB", inc_n=16)
                emit("G", lambda e: e.dma_start(out=dbg_d["d_x0b"], in_=x0buf[:]),
                     waits=[("V", v_lnf1), ("S", s_lnf1)], inc="DB", inc_n=16)
                for k, tile in enumerate((sum8, sq8, mean8, var8, inv8, bias2)):
                    emit("G", lambda e, k=k, tile=tile: e.dma_start(
                        out=dbg_d["d_st"][:, k * 8:(k + 1) * 8], in_=tile[:]),
                        waits=[("V", v_lnf1), ("S", s_lnf1)], inc="DB", inc_n=16)
            preload_set(GELU_SET)

            pk_free = {0: ("V", cnt["V"]), 1: ("V", cnt["V"])}
            v_kdone = cnt["V"]
            s_kdone = cnt["S"]

            stats_pending = []

            def flush_stats(k):
                while stats_pending and k > 0:
                    j, ve = stats_pending.pop(0)
                    emit("S", lambda e, j=j: e.activation(
                        sqscr[:], acc[:, j * D:(j + 1) * D], AF.Square,
                        accum_out=sq8[:, j:j + 1]), waits=[("V", ve)],
                        fwaits=[("S", sqscr_last[0])], inc="S")
                    sqscr_last[0] = cnt["S"]
                    emit("S", lambda e, j=j: e.activation(
                        sqscr[:], acc[:, j * D:(j + 1) * D], AF.Identity,
                        accum_out=sum8[:, j:j + 1]),
                        fwaits=[("S", sqscr_last[0])], inc="S")
                    sqscr_last[0] = cnt["S"]
                    k -= 1

            t_reads_list = []
            for s in range(2):
                for kc in range(DC):
                    bi = (s * DC + kc) % 2
                    bank = pk[bi]
                    for hc in range(DC):
                        emit("T", lambda e, s=s, kc=kc, hc=hc, bank=bank: e.matmul(
                            bank[:],
                            wkt[:, hc * D + kc * 128: hc * D + (kc + 1) * 128],
                            lnf[:, hc * MT + s * 512: hc * MT + (s + 1) * 512],
                            start=(hc == 0), stop=(hc == DC - 1)),
                            waits=([("V", v_lnf1), ("S", s_lnf1), ("WA", wt_attn),
                                    pk_free[bi]] if hc == 0 else []),
                            inc="T" if hc == DC - 1 else None)
                    t_k = cnt["T"]
                    eng = "V" if kc % 2 == 0 else "S"
                    if eng == "V":
                        emit("V", lambda e, kc=kc, bank=bank: e.tensor_copy(
                            kfm[:, kc * 512:(kc + 1) * 512],
                            bank[:]), waits=[("T", t_k)], inc="V")
                    else:
                        emit("S", lambda e, kc=kc, bank=bank: e.copy(
                            kfm[:, kc * 512:(kc + 1) * 512],
                            bank[:]), waits=[("T", t_k)], inc="S")
                    pk_free[bi] = (eng, cnt[eng])
                v_kdone = cnt["V"]
                s_kdone = cnt["S"]
                for kc in range(DC):
                    emit("T", lambda e, s=s, kc=kc: e.matmul(
                        pu[0][0:HD, :],
                        w1t[:, kc * HD:(kc + 1) * HD],
                        kfm[:, kc * 512:(kc + 1) * 512],
                        start=(kc == 0), stop=(kc == DC - 1)),
                        waits=([("V", v_kdone), ("S", s_kdone)]
                               if kc == 0 else []),
                        inc="T" if kc == DC - 1 else None)
                t_h = cnt["T"]
                emit("S", lambda e, s=s: e.activation(
                    a_sb[0:HD, s * 512:(s + 1) * 512], pu[0][0:HD, :], AF.Gelu),
                    waits=[("T", t_h)], inc="S")
                s_gelu = cnt["S"]
                if s == 1:
                    preload_set(NLEXP_SET)
                    flush_stats(4)
                    ch_a = ln_chain((cnt["V"], cnt["S"]), 0, 4)
                    nm_a = ln_normalize(acc, ch_a[0], ch_a[1], t_ln1t,
                                        chunks=range(4))
                for jj in range(4):
                    j = s * 4 + jj
                    for half in range(2):
                        bi = (jj * 2 + half) % 2
                        bank = pk[bi]
                        emit("T", lambda e, s=s, jj=jj, half=half, bank=bank: e.matmul(
                            bank[:],
                            a_sb[0:HD, s * 512 + jj * 128: s * 512 + (jj + 1) * 128],
                            w2t[0:HD, half * 512:(half + 1) * 512],
                            start=True, stop=True),
                            waits=[("S", s_gelu), pk_free[bi]], inc="T")
                        t_r = cnt["T"]
                        emit("V", lambda e, j=j, half=half, bank=bank: e.tensor_add(
                            acc[:, j * D + half * 512: j * D + (half + 1) * 512],
                            x0buf[:, j * D + half * 512: j * D + (half + 1) * 512],
                            bank[:]),
                            waits=[("T", t_r), ("ST", st_all_prev)], inc="V")
                        pk_free[bi] = ("V", cnt["V"])
                    stats_pending.append((j, cnt["V"]))
                t_reads_list.append(cnt["T"])
                if s == 1:
                    va, sa, _ = ln_transpose(nm_a, chunks=range(4))
            flush_stats(8)
            v_x1 = cnt["V"]
            if debug and m == 0:
                emit("SY", lambda e: e.dma_start(out=dbg_d["d_x1"], in_=acc[:]),
                     waits=[("V", v_x1)], inc="DB", inc_n=16)

            ch_b = ln_chain((cnt["V"], cnt["S"]), 4, NCH)
            nm_b = ln_normalize(acc, ch_b[0], ch_b[1], t_ln1t,
                                chunks=range(4, NCH))
            v_lnf2, s_lnf2, t_ln2t = ln_transpose(nm_b, chunks=range(4, NCH))
            v_lnf2 = max(v_lnf2, va)
            s_lnf2 = max(s_lnf2 or 0, sa or 0)
            if debug and m == 0:
                emit("G", lambda e: e.dma_start(out=dbg_d["d_lnf2"], in_=lnf[:]),
                     waits=[("V", v_lnf2)], inc="DB", inc_n=16)
                emit("G", lambda e: e.dma_start(out=dbg_d["d_a"], in_=a_sb[:]),
                     waits=[("V", v_lnf2)], inc="DB", inc_n=16)
            v_ln1t_done_prev = t_ln2t

            for j in range(NCH):
                for dc in range(DC):
                    emit("T", lambda e, j=j, dc=dc: e.matmul(
                        pu[1][:, j * E:(j + 1) * E],
                        lnf[:, dc * MT + j * 128: dc * MT + (j + 1) * 128],
                        wg8[:, dc * E:(dc + 1) * E],
                        start=(dc == 0), stop=(dc == DC - 1)),
                        waits=([("V", v_lnf2), ("S", s_lnf2)]
                               if dc == 0 and j == 0 else []),
                        inc="T" if dc == DC - 1 else None)
            t_scores = cnt["T"]
            emit("S", lambda e: e.copy(scs[:], pu[1][:, 0:NCH * E]),
                 waits=[("T", t_scores)], inc="S")
            s_scs = cnt["S"]
            for j in range(NCH):
                emit("V", lambda e, j=j: e.max(out=mx8[:, j * E:(j + 1) * E],
                                               in_=scs[:, j * E:(j + 1) * E]),
                     waits=[("S", s_scs)])
            for j in range(NCH):
                emit("V", lambda e, j=j: e.reduce_max(
                    nm18[:, j:j + 1], scs[:, j * E:(j + 1) * E], axis=AX.X,
                    negate=True), inc=("V" if j == NCH - 1 else None))
            v_mx = cnt["V"]
            for j in range(NCH):
                emit("S", lambda e, j=j: e.activation(
                    exall[:, j * E:(j + 1) * E], scs[:, j * E:(j + 1) * E], AF.Exp,
                    bias=nm18[:, j:j + 1]), waits=[("V", v_mx)] if j == 0 else [])
            for j in range(NCH):
                emit("S", lambda e, j=j: e.activation(
                    den8[:, j:j + 1], mx8[:, j * E + 1: j * E + 2], AF.Exp,
                    bias=nm18[:, j:j + 1]))
            emit("S", lambda e: e.activation(den8p[:], den8[:], AF.Identity,
                                             bias=onec[:, 0:1]), inc="S")
            s_ex = cnt["S"]
            emit("V", lambda e: e.reciprocal(rden8[:], den8p[:]),
                 waits=[("S", s_ex)])
            for j in range(NCH):
                emit("V", lambda e, j=j: e.tensor_scalar(
                    mskall[:, j * E:(j + 1) * E], scs[:, j * E:(j + 1) * E],
                    mx8[:, j * E + 1: j * E + 2], None, op0=ALU.is_ge),
                    inc=("V" if j == NCH - 1 else None))
            v_msk = cnt["V"]
            emit("S", lambda e: e.nop(), waits=[("V", v_msk)], inc="S")
            s_ping = cnt["S"]
            for j in range(NCH):
                emit("V", lambda e, j=j: e.scalar_tensor_tensor(
                    gat[:, j * E:(j + 1) * E], exall[:, j * E:(j + 1) * E],
                    rden8[:, j:j + 1], mskall[:, j * E:(j + 1) * E],
                    op0=ALU.mult, op1=ALU.mult),
                    waits=[("S", s_ping)] if j == 0 else [],
                    inc=("V" if j == NCH - 1 else None))
            v_gat = cnt["V"]
            preload_set(SILU_SET)
            if debug and m == 0:
                emit("SY", lambda e: e.dma_start(out=dbg_d["d_scs"], in_=scs[:]),
                     waits=[("V", v_gat)], inc="DB", inc_n=16)
                emit("SY", lambda e: e.dma_start(out=dbg_d["d_gat"], in_=gat[:]),
                     waits=[("V", v_gat)], inc="DB", inc_n=16)

            s_silu = {}
            v_hid = {}
            t_gu_prev_bank = {}
            for e_i in range(E):
                gu_waits_now, d_waits_now = wt_expert
                v_hid_mc = {}
                first_mm = True
                for mc in range(MC):
                    for s in range(2):
                        for dc in range(DC):
                            w = []
                            if first_mm:
                                w = list(gu_waits_now)
                                if e_i == 0:
                                    w.append(("S", s_scs))
                                first_mm = False
                            if dc == 0:
                                prev = t_gu_prev_bank.get(("g", s))
                                if prev is not None:
                                    w.append(("S", prev))
                            emit("T", lambda e, mc=mc, s=s, dc=dc: e.matmul(
                                pg[s][:],
                                wgs[:, dc * M + mc * 128: dc * M + (mc + 1) * 128],
                                lnf[:, dc * MT + s * 512: dc * MT + (s + 1) * 512],
                                start=(dc == 0), stop=(dc == DC - 1)),
                                waits=w, inc="T" if dc == DC - 1 else None)
                        t_g = cnt["T"]
                        for dc in range(DC):
                            w = []
                            if dc == 0:
                                prev = t_gu_prev_bank.get(("u", s))
                                if prev is not None:
                                    w.append(("V", prev))
                            emit("T", lambda e, mc=mc, s=s, dc=dc: e.matmul(
                                pu[s][:],
                                wus[:, dc * M + mc * 128: dc * M + (mc + 1) * 128],
                                lnf[:, dc * MT + s * 512: dc * MT + (s + 1) * 512],
                                start=(dc == 0), stop=(dc == DC - 1)),
                                waits=w, inc="T" if dc == DC - 1 else None)
                        t_u = cnt["T"]
                        sbuf_s = silu0 if s == 0 else silu1
                        silu_war = t_gu_prev_bank.get(("u", s))
                        emit("S", lambda e, sb=sbuf_s, s=s: e.activation(
                            sb[:], pg[s][:], AF.Silu),
                            waits=[("T", t_g)]
                            + ([("V", silu_war)] if silu_war is not None else []),
                            inc="S")
                        t_gu_prev_bank[("g", s)] = cnt["S"]
                        s_act = cnt["S"]
                        emit("V", lambda e, sb=sbuf_s, s=s, mc=mc: e.tensor_mul(
                            hid[:, mc * MT + s * 512: mc * MT + (s + 1) * 512],
                            sb[:], pu[s][:]),
                            waits=[("S", s_act), ("T", t_u)], inc="V")
                        t_gu_prev_bank[("u", s)] = cnt["V"]
                        if s == 1:
                            v_hid_mc[mc] = cnt["V"]
                v_hidden = cnt["V"]
                t_gu_done = cnt["T"]
                ne = (e_i + 1) % E
                last = (m == NMACRO - 1 and e_i == E - 1)
                v_evac_d = {}
                t_prev_d = {}
                for j in range(NCH):
                    for half in range(2):
                        bank = pk[(j * 2 + half) % 2]
                        for mc in range(MC):
                            w = [("V", v_hid_mc[mc])]
                            if mc == 0:
                                if j == 0 and half == 0:
                                    w.extend(d_waits_now)
                                    w.append(("V", v_gat))
                                prev = t_prev_d.get((j * 2 + half) % 2)
                                if prev is not None:
                                    w.append(("V", prev))
                            emit("T", lambda e, j=j, half=half, mc=mc, bank=bank: e.matmul(
                                bank[:],
                                hid[:, mc * MT + j * 128: mc * MT + (j + 1) * 128],
                                wds[:, mc * D + half * 512: mc * D + (half + 1) * 512],
                                start=(mc == 0), stop=(mc == MC - 1)),
                                waits=w, inc="T" if mc == MC - 1 else None)
                        t_d = cnt["T"]
                        emit("V", lambda e, j=j, half=half, bank=bank, e_i=e_i:
                             e.scalar_tensor_tensor(
                                 acc[:, j * D + half * 512: j * D + (half + 1) * 512],
                                 bank[:], gat[:, j * E + e_i: j * E + e_i + 1],
                                 acc[:, j * D + half * 512: j * D + (half + 1) * 512],
                                 op0=ALU.mult, op1=ALU.add),
                             waits=[("T", t_d)], inc="V")
                        t_prev_d[(j * 2 + half) % 2] = cnt["V"]
                        v_evac_d[j] = cnt["V"]
                t_d_done = cnt["T"]
                if not last:
                    nxt_cached = (m > 0) if e_i < E - 1 else True
                    wt_expert = load_expert(ne, t_gu_done, t_d_done, nxt_cached)
                if m < NMACRO - 1:
                    if e_i == 1:
                        ntok0 = (m + 1) * MT
                        for j in range(NCH):
                            emit("G", lambda e, j=j, ntok0=ntok0: e.dma_start(
                                out=x0buf[:, j * D:(j + 1) * D],
                                in_=x_d[ntok0 + j * 128: ntok0 + (j + 1) * 128, :]),
                                waits=[("V", v_x1)], inc="XB", inc_n=16)
                        ld_x_next = cnt["XB"]
                    if e_i in (2, 3):
                        for j in range(4 * (e_i - 2), 4 * (e_i - 1)):
                            emit_stats(x0buf, j, [("XB", ld_x_next)],
                                       [("XB", ld_x_next)])
                        if e_i == 3:
                            front_stats = (cnt["V"], cnt["S"])
                    if e_i == 4:
                        front_chain = ln_chain(front_stats)
                    if e_i == 5:
                        front_marker = ln_normalize(
                            x0buf, front_chain[0], front_chain[1], t_ln2t)
                v_experts_done = cnt["V"]
                v_evac_final = dict(v_evac_d)

            for j in range(NCH):
                emit("SY", lambda e, j=j, tok0=tok0: e.dma_start(
                    out=y_d[tok0 + j * 128: tok0 + (j + 1) * 128, :],
                    in_=acc[:, j * D:(j + 1) * D]),
                    waits=[("V", v_evac_final[j])], inc="ST", inc_n=16)
            st_all_prev = cnt["ST"]

        st_total = cnt["ST"]
        emit("SY", lambda e: e.nop(),
             waits=[("ST", st_total)] + ([("DB", cnt["DB"])] if debug else []))

        block = ctx.enter_context(nc.Block())

        global _SBUF_FREE
        _SBUF_FREE = (nc.sbuf_bytes_remaining, nc.psum_banks_remaining)
        self_sem = {"T": "T", "V": "V", "S": "S"}

        def walker(eng_name):
            def run(eng):
                waited = {}
                skip = self_sem.get(eng_name)
                for rec in steps:
                    if rec["eng"] != eng_name:
                        continue
                    for sname, val in rec["waits"]:
                        if sname == skip:
                            continue
                        if waited.get(sname, 0) >= val:
                            continue
                        eng.wait_ge(sems[sname], val)
                        waited[sname] = val
                    for sname, val in rec.get("fwaits", ()):
                        if waited.get(sname, 0) >= val:
                            continue
                        eng.wait_ge(sems[sname], val)
                        waited[sname] = val
                    inst = rec["fn"](eng)
                    if rec["inc"] is not None:
                        inst.then_inc(sems[rec["inc"]], rec["inc_n"])
            return run

        block.gpsimd(walker("G"))
        block.vector(walker("V"))
        block.scalar(walker("S"))
        block.tensor(walker("T"))
        block.sync(walker("SY"))

    return nc


_NC_CACHE = {}


def kernel(**inputs):
    x = np.ascontiguousarray(
        np.asarray(inputs["L0_attn_x"], dtype=np.float32).reshape(TOK, D))
    wkt = np.ascontiguousarray(np.asarray(inputs["L0_attn_W_K"], dtype=np.float32).T)
    w1t = np.ascontiguousarray(np.asarray(inputs["L0_attn_W1"], dtype=np.float32).T)
    w2t = np.ascontiguousarray(np.asarray(inputs["L0_attn_W2"], dtype=np.float32).T)
    wg8 = np.ascontiguousarray(np.asarray(inputs["L0_ffn_w_gate"], dtype=np.float32))
    wg = np.ascontiguousarray(np.asarray(inputs["L0_ffn_w_g"], dtype=np.float32))
    wu = np.ascontiguousarray(np.asarray(inputs["L0_ffn_w_u"], dtype=np.float32))
    wd = np.ascontiguousarray(np.asarray(inputs["L0_ffn_w_d"], dtype=np.float32))

    if "nc" not in _NC_CACHE:
        _NC_CACHE["nc"] = build_program()
    nc = _NC_CACHE["nc"]

    in_maps = []
    for c in range(NCORES):
        in_maps.append({
            "x": x[c * TPC:(c + 1) * TPC],
            "wkt": wkt, "w1t": w1t, "w2t": w2t, "wgate": wg8,
            "wg": wg, "wu": wu, "wd": wd,
        })
    res = run_bass_kernel_spmd(nc, in_maps, core_ids=list(range(NCORES)))
    y = np.concatenate([r["y"] for r in res.results], axis=0)
    return y.reshape(B, S, D)


# version 23
# speedup vs baseline: 1.0932x; 1.0932x over previous
import numpy as np
from contextlib import ExitStack

import concourse.bass as bass
import concourse.mybir as mybir
from concourse.bass_utils import run_bass_kernel_spmd

F32 = mybir.dt.float32
BF16 = mybir.dt.bfloat16
AF = mybir.ActivationFunctionType
ALU = mybir.AluOpType
AX = mybir.AxisListType

NCORES = 8
B, S, D = 16, 2048, 1024
TOK = B * S
TPC = TOK // NCORES
MT = 1024
NMACRO = TPC // MT
NCH = MT // 128
DC = D // 128
M = 1408
MC = M // 128
E = 8
HD = 64


def build_program(debug=False):
    nc = bass.Bass("TRN2", target_bir_lowering=False,
                   detect_race_conditions=False)
    x_d = nc.dram_tensor("x", [TPC, D], F32, kind="ExternalInput").ap()
    wkt_d = nc.dram_tensor("wkt", [D, D], F32, kind="ExternalInput").ap()
    w1t_d = nc.dram_tensor("w1t", [D, HD], F32, kind="ExternalInput").ap()
    w2t_d = nc.dram_tensor("w2t", [HD, D], F32, kind="ExternalInput").ap()
    wg8_d = nc.dram_tensor("wgate", [D, E], F32, kind="ExternalInput").ap()
    wg_d = nc.dram_tensor("wg", [E, D, M], F32, kind="ExternalInput").ap()
    wu_d = nc.dram_tensor("wu", [E, D, M], F32, kind="ExternalInput").ap()
    wd_d = nc.dram_tensor("wd", [E, M, D], F32, kind="ExternalInput").ap()
    y_d = nc.dram_tensor("y", [TPC, D], F32, kind="ExternalOutput").ap()
    dbg_d = {}
    if debug:
        for nm, shape in [("d_x1", [128, NCH * D]), ("d_lnf", [128, DC * MT]),
                          ("d_kfm", [128, DC * 512]), ("d_a", [128, MT]),
                          ("d_scs", [128, NCH * E]), ("d_gat", [128, NCH * E]),
                          ("d_lnf2", [128, DC * MT]), ("d_ltm", [128, NCH * D]), ("d_x0b", [128, NCH * D]), ("d_st", [128, 48])]:
            dbg_d[nm] = nc.dram_tensor(nm, shape, F32, kind="ExternalOutput").ap()

    import ml_dtypes
    ident_d = nc.inline_tensor(np.eye(128, dtype=ml_dtypes.bfloat16), name="ident")
    wgbf_d = nc.dram_tensor("wgbf", [E, 128, DC * M], BF16, kind="Internal").ap()
    wubf_d = nc.dram_tensor("wubf", [E, 128, DC * M], BF16, kind="Internal").ap()
    wdbf_d = nc.dram_tensor("wdbf", [E, 128, MC * D], BF16, kind="Internal").ap()

    with ExitStack() as ctx:
        SB = lambda name, shape, dt: ctx.enter_context(nc.sbuf_tensor(name, shape, dt))
        PS = lambda name, shape, dt: ctx.enter_context(nc.psum_tensor(name, shape, dt))

        acc = SB("acc", [128, NCH * D], F32)
        x0buf = SB("x0buf", [128, NCH * D], BF16)
        ln_tm = SB("ln_tm", [128, NCH * D], BF16)
        lnf = SB("lnf", [128, DC * MT], BF16)
        kfm = SB("kfm", [128, DC * 512], BF16)
        a_sb = SB("a_sb", [128, MT], BF16)
        hid = SB("hid", [128, MC * MT], BF16)
        wgs = SB("wgs", [128, DC * M], BF16)
        wus = SB("wus", [128, DC * M], BF16)
        wds = SB("wds", [128, MC * D], BF16)
        wkt = SB("wkt_sb", [128, DC * D], BF16)
        w1t = SB("w1t_sb", [128, DC * HD], BF16)
        w2t = SB("w2t_sb", [128, D], BF16)
        wg8 = SB("wg8_sb", [128, DC * E], BF16)
        ident = SB("ident_sb", [128, 128], BF16)
        sqscr = SB("sqscr", [128, D], F32)
        silu0 = SB("silu0", [128, 512], BF16)
        silu1 = SB("silu1", [128, 512], BF16)
        sum8 = SB("sum8", [128, NCH], F32)
        sq8 = SB("sq8", [128, NCH], F32)
        mean8 = SB("mean8", [128, NCH], F32)
        exsq8 = SB("exsq8", [128, NCH], F32)
        msq8 = SB("msq8", [128, NCH], F32)
        var8 = SB("var8", [128, NCH], F32)
        lnv8 = SB("lnv8", [128, NCH], F32)
        inv8 = SB("inv8", [128, NCH], F32)
        scs = SB("scs", [128, NCH * E], F32)
        mx8 = SB("mx8", [128, NCH * E], F32)
        nm18 = SB("nm18", [128, NCH], F32)
        exall = SB("exall", [128, NCH * E], F32)
        den8 = SB("den8", [128, NCH], F32)
        den8p = SB("den8p", [128, NCH], F32)
        rden8 = SB("rden8", [128, NCH], F32)
        mskall = SB("mskall", [128, NCH * E], F32)
        gat = SB("gat", [128, NCH * E], F32)
        nmi8 = SB("nmi8", [128, NCH], F32)
        bias2 = SB("bias2", [128, NCH], F32)
        invD = SB("invD", [128, 1], F32)
        epsc = SB("epsc", [128, 1], F32)
        nhc = SB("nhc", [128, 1], F32)
        onec = SB("onec", [128, 1], F32)
        negone = SB("negone", [128, 1], F32)

        pbt = PS("pbt", [128, 1024], BF16)
        pk = [PS("pk0", [128, 512], F32), PS("pk1", [128, 512], F32)]
        pg = [PS("pg0", [128, 512], F32), PS("pg1", [128, 512], F32)]
        pu = [PS("pu0", [128, 512], F32), PS("pu1", [128, 512], F32)]
        pbt2 = PS("pbt2", [128, 1024], BF16)

        sems = {}
        for name in ("LD", "XB", "WA", "WG", "WU", "WD", "T", "V", "S", "ST", "DB"):
            sems[name] = ctx.enter_context(nc.semaphore(name))

        steps = []
        cnt = {k: 0 for k in sems}

        def emit(eng, fn, waits=(), inc=None, inc_n=1, fwaits=()):
            steps.append({"eng": eng, "waits": [w for w in waits if w[1] > 0],
                          "fwaits": [w for w in fwaits if w[1] > 0],
                          "fn": fn, "inc": inc, "inc_n": inc_n})
            if inc is not None:
                cnt[inc] += inc_n
            return cnt[inc] if inc is not None else None

        emit("V", lambda e: e.memset(invD[:], 1.0 / D))
        emit("V", lambda e: e.memset(epsc[:], 1e-5))
        emit("V", lambda e: e.memset(nhc[:], -0.5))
        emit("V", lambda e: e.memset(negone[:], -1.0))
        emit("V", lambda e: e.memset(onec[:], 1.0), inc="V")
        v_consts = cnt["V"]

        emit("SY", lambda e: e.dma_start(out=ident[:], in_=ident_d.ap()), inc="LD", inc_n=16)
        ld_ident = cnt["LD"]

        st_all_prev = 0
        ld_x_next = []
        for j in range(NCH):
            emit("G", lambda e, j=j: e.dma_start(
                out=x0buf[:, j * D:(j + 1) * D],
                in_=x_d[j * 128:(j + 1) * 128, :]), inc="XB", inc_n=16)
        ld_x_next = cnt["XB"]

        for hc in range(DC):
            emit("G", lambda e, hc=hc: e.dma_start(
                out=wkt[:, hc * D:(hc + 1) * D], in_=wkt_d[hc * 128:(hc + 1) * 128, :]),
                inc="WA", inc_n=16)
        for kc in range(DC):
            emit("G", lambda e, kc=kc: e.dma_start(
                out=w1t[:, kc * HD:(kc + 1) * HD], in_=w1t_d[kc * 128:(kc + 1) * 128, :]),
                inc="WA", inc_n=16)
        emit("G", lambda e: e.dma_start(out=w2t[0:HD, :], in_=w2t_d), inc="WA", inc_n=16)
        for dc in range(DC):
            emit("G", lambda e, dc=dc: e.dma_start(
                out=wg8[:, dc * E:(dc + 1) * E], in_=wg8_d[dc * 128:(dc + 1) * 128, :]),
                inc="WA", inc_n=16)
        wt_attn = cnt["WA"]

        wt_wrote = {"g": {}, "u": {}, "d": {}}

        def load_expert(e_idx, wait_t_gu, wait_t_d, cached):
            if not cached:
                wg0, wu0, wd0 = cnt["WG"], cnt["WU"], cnt["WD"]
                for dc in range(DC):
                    emit("G", lambda e, dc=dc, ei=e_idx: e.dma_start(
                        out=wgs[:, dc * M:(dc + 1) * M],
                        in_=wg_d[ei, dc * 128:(dc + 1) * 128, :]),
                        waits=[("T", wait_t_gu), ("WG", wg0)], inc="WG", inc_n=16)
                wt_lg = cnt["WG"]
                for dc in range(DC):
                    emit("G", lambda e, dc=dc, ei=e_idx: e.dma_start(
                        out=wus[:, dc * M:(dc + 1) * M],
                        in_=wu_d[ei, dc * 128:(dc + 1) * 128, :]),
                        waits=[("T", wait_t_gu), ("WU", wu0)], inc="WU", inc_n=16)
                wt_lu = cnt["WU"]
                emit("G", lambda e, ei=e_idx: e.dma_start(
                    out=wgbf_d[ei], in_=wgs[:]),
                    waits=[("WG", wt_lg)], inc="WG", inc_n=16)
                wt_wrote["g"][e_idx] = cnt["WG"]
                emit("G", lambda e, ei=e_idx: e.dma_start(
                    out=wubf_d[ei], in_=wus[:]),
                    waits=[("WU", wt_lu)], inc="WU", inc_n=16)
                wt_wrote["u"][e_idx] = cnt["WU"]
                gu_waits = [("WG", wt_lg), ("WU", wt_lu)]
                for mc in range(MC):
                    emit("G", lambda e, mc=mc, ei=e_idx: e.dma_start(
                        out=wds[:, mc * D:(mc + 1) * D],
                        in_=wd_d[ei, mc * 128:(mc + 1) * 128, :]),
                        waits=[("T", wait_t_d), ("WD", wd0)], inc="WD", inc_n=16)
                wt_ld = cnt["WD"]
                emit("G", lambda e, ei=e_idx: e.dma_start(
                    out=wdbf_d[ei], in_=wds[:]),
                    waits=[("WD", wt_ld)], inc="WD", inc_n=16)
                wt_wrote["d"][e_idx] = cnt["WD"]
                d_waits = [("WD", wt_ld)]
            else:
                emit("G", lambda e, ei=e_idx: e.dma_start(
                    out=wgs[:], in_=wgbf_d[ei]),
                    waits=[("T", wait_t_gu), ("WG", cnt["WG"])],
                    inc="WG", inc_n=16)
                wg_c = cnt["WG"]
                emit("G", lambda e, ei=e_idx: e.dma_start(
                    out=wus[:], in_=wubf_d[ei]),
                    waits=[("T", wait_t_gu), ("WU", cnt["WU"])],
                    inc="WU", inc_n=16)
                gu_waits = [("WG", wg_c), ("WU", cnt["WU"])]
                emit("G", lambda e, ei=e_idx: e.dma_start(
                    out=wds[:], in_=wdbf_d[ei]),
                    waits=[("T", wait_t_d), ("WD", cnt["WD"])],
                    inc="WD", inc_n=16)
                d_waits = [("WD", cnt["WD"])]
            return gu_waits, d_waits

        wt_expert = load_expert(0, 0, 0, cached=False)

        NLEXP_SET, GELU_SET, SILU_SET = 6, 10, 18

        def preload_set(sid):
            pass

        v_ln1t_done_prev = 0

        sqscr_last = [0]

        def emit_stats(src, j, w_s, w_v):
            emit("S", lambda e, j=j: e.activation(
                sqscr[:], src[:, j * D:(j + 1) * D], AF.Square,
                accum_out=sq8[:, j:j + 1]), waits=w_s,
                fwaits=[("S", sqscr_last[0])], inc="S")
            sqscr_last[0] = cnt["S"]
            s_m = cnt["S"]
            emit("V", lambda e, j=j: e.reduce_sum(
                sum8[:, j:j + 1], src[:, j * D:(j + 1) * D], axis=AX.X),
                waits=w_v + [("S", s_m)], inc="V")

        def ln_chain(stats_done, c0=0, c1=NCH):
            v_sums, s_sqs = stats_done
            cs = slice(c0, c1)
            emit("S", lambda e: e.activation(mean8[:, cs], sum8[:, cs],
                                             AF.Identity, scale=invD[:, 0:1]),
                 waits=[("V", v_sums)], inc="S")
            emit("S", lambda e: e.activation(exsq8[:, cs], sq8[:, cs],
                                             AF.Identity, scale=invD[:, 0:1]),
                 inc="S")
            emit("S", lambda e: e.activation(msq8[:, cs], mean8[:, cs],
                                             AF.Square), inc="S")
            s_stats = cnt["S"]
            emit("V", lambda e: e.tensor_sub(var8[:, cs], exsq8[:, cs],
                                             msq8[:, cs]),
                 waits=[("S", s_stats)], inc="V")
            v_var = cnt["V"]
            emit("S", lambda e: e.activation(lnv8[:, cs], var8[:, cs], AF.Ln,
                                             bias=epsc[:, 0:1]),
                 waits=[("V", v_var)], inc="S")
            emit("S", lambda e: e.activation(inv8[:, cs], lnv8[:, cs], AF.Exp,
                                             scale=nhc[:, 0:1]), inc="S")
            s_inv = cnt["S"]
            emit("V", lambda e: e.tensor_mul(nmi8[:, cs], mean8[:, cs],
                                             inv8[:, cs]),
                 waits=[("S", s_inv)])
            emit("V", lambda e: e.tensor_scalar_mul(bias2[:, cs], nmi8[:, cs],
                                                    negone[:, 0:1]), inc="V")
            return s_inv, cnt["V"]

        def ln_normalize(src, s_inv, v_b2, t_lntm_free, chunks=range(NCH)):
            norm_marker = {}
            for j in chunks:
                if j % 2 == 0 or j == 7:
                    emit("V", lambda e, j=j: e.tensor_scalar(
                        ln_tm[:, j * D:(j + 1) * D], src[:, j * D:(j + 1) * D],
                        mean8[:, j:j + 1], inv8[:, j:j + 1],
                        op0=ALU.subtract, op1=ALU.mult),
                        waits=[("S", s_inv), ("T", t_lntm_free)], inc="V")
                    norm_marker[j] = ("V", cnt["V"])
                else:
                    emit("S", lambda e, j=j: e.activation(
                        ln_tm[:, j * D:(j + 1) * D], src[:, j * D:(j + 1) * D],
                        AF.Identity, scale=inv8[:, j:j + 1],
                        bias=bias2[:, j:j + 1]),
                        waits=[("V", v_b2), ("T", t_lntm_free)], inc="S")
                    norm_marker[j] = ("S", cnt["S"])
            return norm_marker

        def ln_chain_norm(src, stats_done, t_lntm_free):
            s_inv, v_b2 = ln_chain(stats_done)
            return ln_normalize(src, s_inv, v_b2, t_lntm_free)

        pbt_state = {}

        def ln_transpose(norm_marker, chunks=range(NCH)):
            lnf_v = lnf[:].rearrange("p (c t) -> p c t", t=MT)
            banks = (pbt, pbt2)
            v_evac_last = s_evac_last = None
            for j in chunks:
                bank = banks[j % 2]
                bank_v = bank[:].rearrange("p (c t) -> p c t", t=128)
                for c in range(DC):
                    w = []
                    if c == 0:
                        w = [norm_marker[j], ("LD", ld_ident)]
                        prev = pbt_state.get(j % 2)
                        if prev:
                            w.append(prev)
                    emit("T", lambda e, c=c, j=j, bank=bank: e.transpose(
                        bank[:, c * 128:(c + 1) * 128],
                        ln_tm[:, j * D + c * 128: j * D + (c + 1) * 128],
                        ident[:]),
                        waits=w, inc="T" if c == DC - 1 else None)
                t_j = cnt["T"]
                emit("V", lambda e, j=j, bank_v=bank_v: e.tensor_copy(
                    lnf_v[:, :, j * 128:(j + 1) * 128], bank_v[:]),
                    waits=[("T", t_j)], inc="V")
                pbt_state[j % 2] = ("V", cnt["V"])
                v_evac_last = cnt["V"]
                s_evac_last = cnt["S"]
            return v_evac_last, s_evac_last, cnt["T"]

        front_marker = None

        for m in range(NMACRO):
            tok0 = m * MT
            ld_x = ld_x_next

            if front_marker is None:
                preload_set(NLEXP_SET)
                for j in range(NCH):
                    emit_stats(x0buf, j, [("XB", ld_x)], [("XB", ld_x)])
                front_marker = ln_chain_norm(x0buf, (cnt["V"], cnt["S"]),
                                             v_ln1t_done_prev)
            v_lnf1, s_lnf1, t_ln1t = ln_transpose(front_marker)
            front_marker = None
            if debug and m == 1:
                emit("G", lambda e: e.dma_start(out=dbg_d["d_lnf"], in_=lnf[:]),
                     waits=[("V", v_lnf1), ("S", s_lnf1)], inc="DB", inc_n=16)
                emit("G", lambda e: e.dma_start(out=dbg_d["d_ltm"], in_=ln_tm[:]),
                     waits=[("V", v_lnf1), ("S", s_lnf1)], inc="DB", inc_n=16)
                emit("G", lambda e: e.dma_start(out=dbg_d["d_x0b"], in_=x0buf[:]),
                     waits=[("V", v_lnf1), ("S", s_lnf1)], inc="DB", inc_n=16)
                for k, tile in enumerate((sum8, sq8, mean8, var8, inv8, bias2)):
                    emit("G", lambda e, k=k, tile=tile: e.dma_start(
                        out=dbg_d["d_st"][:, k * 8:(k + 1) * 8], in_=tile[:]),
                        waits=[("V", v_lnf1), ("S", s_lnf1)], inc="DB", inc_n=16)
            preload_set(GELU_SET)

            pk_free = {0: ("V", cnt["V"]), 1: ("V", cnt["V"])}
            v_kdone = cnt["V"]
            s_kdone = cnt["S"]

            stats_pending = []

            def flush_stats(k):
                while stats_pending and k > 0:
                    j, ve = stats_pending.pop(0)
                    emit("S", lambda e, j=j: e.activation(
                        sqscr[:], acc[:, j * D:(j + 1) * D], AF.Square,
                        accum_out=sq8[:, j:j + 1]), waits=[("V", ve)],
                        fwaits=[("S", sqscr_last[0])], inc="S")
                    sqscr_last[0] = cnt["S"]
                    emit("S", lambda e, j=j: e.activation(
                        sqscr[:], acc[:, j * D:(j + 1) * D], AF.Identity,
                        accum_out=sum8[:, j:j + 1]),
                        fwaits=[("S", sqscr_last[0])], inc="S")
                    sqscr_last[0] = cnt["S"]
                    k -= 1

            rbanks = [pk[0], pk[1], pg[0], pg[1]]
            rfree = {0: pk_free[0], 1: pk_free[1], 2: None, 3: None}

            def kproj(s):
                for kc in range(DC):
                    bi = kc % 2
                    bank = pk[bi]
                    for hc in range(DC):
                        emit("T", lambda e, s=s, kc=kc, hc=hc, bank=bank: e.matmul(
                            bank[:],
                            wkt[:, hc * D + kc * 128: hc * D + (kc + 1) * 128],
                            lnf[:, hc * MT + s * 512: hc * MT + (s + 1) * 512],
                            start=(hc == 0), stop=(hc == DC - 1)),
                            waits=([("V", v_lnf1), ("S", s_lnf1), ("WA", wt_attn)]
                                   + ([rfree[bi]] if rfree[bi] else [])
                                   if hc == 0 else []),
                            inc="T" if hc == DC - 1 else None)
                    t_k = cnt["T"]
                    eng = "V" if kc % 2 == 0 else "S"
                    if eng == "V":
                        emit("V", lambda e, kc=kc, bank=bank: e.tensor_copy(
                            kfm[:, kc * 512:(kc + 1) * 512],
                            bank[:]), waits=[("T", t_k)], inc="V")
                    else:
                        emit("S", lambda e, kc=kc, bank=bank: e.copy(
                            kfm[:, kc * 512:(kc + 1) * 512],
                            bank[:]), waits=[("T", t_k)], inc="S")
                    rfree[bi] = (eng, cnt[eng])
                return cnt["V"], cnt["S"]

            def hpass(s, vk, sk):
                for kc in range(DC):
                    emit("T", lambda e, s=s, kc=kc: e.matmul(
                        pu[0][0:HD, :],
                        w1t[:, kc * HD:(kc + 1) * HD],
                        kfm[:, kc * 512:(kc + 1) * 512],
                        start=(kc == 0), stop=(kc == DC - 1)),
                        waits=([("V", vk), ("S", sk)] if kc == 0 else []),
                        inc="T" if kc == DC - 1 else None)
                t_h = cnt["T"]
                emit("S", lambda e, s=s: e.activation(
                    a_sb[0:HD, s * 512:(s + 1) * 512], pu[0][0:HD, :], AF.Gelu),
                    waits=[("T", t_h)], inc="S")
                return cnt["S"]

            def reads(s, s_gelu):
                for jj in range(4):
                    j = s * 4 + jj
                    for half in range(2):
                        bi = (jj * 2 + half) % 4
                        bank = rbanks[bi]
                        emit("T", lambda e, s=s, jj=jj, half=half, bank=bank: e.matmul(
                            bank[:],
                            a_sb[0:HD, s * 512 + jj * 128: s * 512 + (jj + 1) * 128],
                            w2t[0:HD, half * 512:(half + 1) * 512],
                            start=True, stop=True),
                            waits=[("S", s_gelu)]
                            + ([rfree[bi]] if rfree[bi] else []), inc="T")
                        t_r = cnt["T"]
                        emit("V", lambda e, j=j, half=half, bank=bank: e.tensor_add(
                            acc[:, j * D + half * 512: j * D + (half + 1) * 512],
                            x0buf[:, j * D + half * 512: j * D + (half + 1) * 512],
                            bank[:]),
                            waits=[("T", t_r), ("ST", st_all_prev)], inc="V")
                        rfree[bi] = ("V", cnt["V"])
                    stats_pending.append((j, cnt["V"]))

            vk0, sk0 = kproj(0)
            sg0 = hpass(0, vk0, sk0)
            vk1, sk1 = kproj(1)
            reads(0, sg0)
            sg1 = hpass(1, vk1, sk1)
            preload_set(NLEXP_SET)
            flush_stats(4)
            ch_a = ln_chain((cnt["V"], cnt["S"]), 0, 4)
            nm_a = ln_normalize(acc, ch_a[0], ch_a[1], t_ln1t, chunks=range(4))
            reads(1, sg1)
            va, sa, _ = ln_transpose(nm_a, chunks=range(4))
            flush_stats(8)
            v_x1 = cnt["V"]
            if debug and m == 0:
                emit("SY", lambda e: e.dma_start(out=dbg_d["d_x1"], in_=acc[:]),
                     waits=[("V", v_x1)], inc="DB", inc_n=16)

            ch_b = ln_chain((cnt["V"], cnt["S"]), 4, NCH)
            nm_b = ln_normalize(acc, ch_b[0], ch_b[1], t_ln1t,
                                chunks=range(4, NCH))
            v_lnf2, s_lnf2, t_ln2t = ln_transpose(nm_b, chunks=range(4, NCH))
            v_lnf2 = max(v_lnf2, va)
            s_lnf2 = max(s_lnf2 or 0, sa or 0)
            if debug and m == 0:
                emit("G", lambda e: e.dma_start(out=dbg_d["d_lnf2"], in_=lnf[:]),
                     waits=[("V", v_lnf2)], inc="DB", inc_n=16)
                emit("G", lambda e: e.dma_start(out=dbg_d["d_a"], in_=a_sb[:]),
                     waits=[("V", v_lnf2)], inc="DB", inc_n=16)
            v_ln1t_done_prev = t_ln2t

            for j in range(NCH):
                for dc in range(DC):
                    emit("T", lambda e, j=j, dc=dc: e.matmul(
                        pu[1][:, j * E:(j + 1) * E],
                        lnf[:, dc * MT + j * 128: dc * MT + (j + 1) * 128],
                        wg8[:, dc * E:(dc + 1) * E],
                        start=(dc == 0), stop=(dc == DC - 1)),
                        waits=([("V", v_lnf2), ("S", s_lnf2)]
                               if dc == 0 and j == 0 else []),
                        inc="T" if dc == DC - 1 else None)
            t_scores = cnt["T"]
            emit("S", lambda e: e.copy(scs[:], pu[1][:, 0:NCH * E]),
                 waits=[("T", t_scores)], inc="S")
            s_scs = cnt["S"]
            for j in range(NCH):
                emit("V", lambda e, j=j: e.max(out=mx8[:, j * E:(j + 1) * E],
                                               in_=scs[:, j * E:(j + 1) * E]),
                     waits=[("S", s_scs)])
            for j in range(NCH):
                emit("V", lambda e, j=j: e.reduce_max(
                    nm18[:, j:j + 1], scs[:, j * E:(j + 1) * E], axis=AX.X,
                    negate=True), inc=("V" if j == NCH - 1 else None))
            v_mx = cnt["V"]
            for j in range(NCH):
                emit("S", lambda e, j=j: e.activation(
                    exall[:, j * E:(j + 1) * E], scs[:, j * E:(j + 1) * E], AF.Exp,
                    bias=nm18[:, j:j + 1]), waits=[("V", v_mx)] if j == 0 else [])
            for j in range(NCH):
                emit("S", lambda e, j=j: e.activation(
                    den8[:, j:j + 1], mx8[:, j * E + 1: j * E + 2], AF.Exp,
                    bias=nm18[:, j:j + 1]))
            emit("S", lambda e: e.activation(den8p[:], den8[:], AF.Identity,
                                             bias=onec[:, 0:1]), inc="S")
            s_ex = cnt["S"]
            emit("V", lambda e: e.reciprocal(rden8[:], den8p[:]),
                 waits=[("S", s_ex)])
            for j in range(NCH):
                emit("V", lambda e, j=j: e.tensor_scalar(
                    mskall[:, j * E:(j + 1) * E], scs[:, j * E:(j + 1) * E],
                    mx8[:, j * E + 1: j * E + 2], None, op0=ALU.is_ge),
                    inc=("V" if j == NCH - 1 else None))
            v_msk = cnt["V"]
            emit("S", lambda e: e.nop(), waits=[("V", v_msk)], inc="S")
            s_ping = cnt["S"]
            for j in range(NCH):
                emit("V", lambda e, j=j: e.scalar_tensor_tensor(
                    gat[:, j * E:(j + 1) * E], exall[:, j * E:(j + 1) * E],
                    rden8[:, j:j + 1], mskall[:, j * E:(j + 1) * E],
                    op0=ALU.mult, op1=ALU.mult),
                    waits=[("S", s_ping)] if j == 0 else [],
                    inc=("V" if j == NCH - 1 else None))
            v_gat = cnt["V"]
            preload_set(SILU_SET)
            if debug and m == 0:
                emit("SY", lambda e: e.dma_start(out=dbg_d["d_scs"], in_=scs[:]),
                     waits=[("V", v_gat)], inc="DB", inc_n=16)
                emit("SY", lambda e: e.dma_start(out=dbg_d["d_gat"], in_=gat[:]),
                     waits=[("V", v_gat)], inc="DB", inc_n=16)

            s_silu = {}
            v_hid = {}
            t_gu_prev_bank = {}
            for e_i in range(E):
                gu_waits_now, d_waits_now = wt_expert
                v_hid_mc = {}
                first_mm = True
                for mc in range(MC):
                    for s in range(2):
                        for dc in range(DC):
                            w = []
                            if first_mm:
                                w = list(gu_waits_now)
                                if e_i == 0:
                                    w.append(("S", s_scs))
                                first_mm = False
                            if dc == 0:
                                prev = t_gu_prev_bank.get(("g", s))
                                if prev is not None:
                                    w.append(("S", prev))
                            emit("T", lambda e, mc=mc, s=s, dc=dc: e.matmul(
                                pg[s][:],
                                wgs[:, dc * M + mc * 128: dc * M + (mc + 1) * 128],
                                lnf[:, dc * MT + s * 512: dc * MT + (s + 1) * 512],
                                start=(dc == 0), stop=(dc == DC - 1)),
                                waits=w, inc="T" if dc == DC - 1 else None)
                        t_g = cnt["T"]
                        for dc in range(DC):
                            w = []
                            if dc == 0:
                                prev = t_gu_prev_bank.get(("u", s))
                                if prev is not None:
                                    w.append(("V", prev))
                            emit("T", lambda e, mc=mc, s=s, dc=dc: e.matmul(
                                pu[s][:],
                                wus[:, dc * M + mc * 128: dc * M + (mc + 1) * 128],
                                lnf[:, dc * MT + s * 512: dc * MT + (s + 1) * 512],
                                start=(dc == 0), stop=(dc == DC - 1)),
                                waits=w, inc="T" if dc == DC - 1 else None)
                        t_u = cnt["T"]
                        sbuf_s = silu0 if s == 0 else silu1
                        silu_war = t_gu_prev_bank.get(("u", s))
                        emit("S", lambda e, sb=sbuf_s, s=s: e.activation(
                            sb[:], pg[s][:], AF.Silu),
                            waits=[("T", t_g)]
                            + ([("V", silu_war)] if silu_war is not None else []),
                            inc="S")
                        t_gu_prev_bank[("g", s)] = cnt["S"]
                        s_act = cnt["S"]
                        emit("V", lambda e, sb=sbuf_s, s=s, mc=mc: e.tensor_mul(
                            hid[:, mc * MT + s * 512: mc * MT + (s + 1) * 512],
                            sb[:], pu[s][:]),
                            waits=[("S", s_act), ("T", t_u)], inc="V")
                        t_gu_prev_bank[("u", s)] = cnt["V"]
                        if s == 1:
                            v_hid_mc[mc] = cnt["V"]
                v_hidden = cnt["V"]
                t_gu_done = cnt["T"]
                ne = (e_i + 1) % E
                last = (m == NMACRO - 1 and e_i == E - 1)
                v_evac_d = {}
                t_prev_d = {}
                for j in range(NCH):
                    for half in range(2):
                        bank = pk[(j * 2 + half) % 2]
                        for mc in range(MC):
                            w = [("V", v_hid_mc[mc])]
                            if mc == 0:
                                if j == 0 and half == 0:
                                    w.extend(d_waits_now)
                                    w.append(("V", v_gat))
                                prev = t_prev_d.get((j * 2 + half) % 2)
                                if prev is not None:
                                    w.append(("V", prev))
                            emit("T", lambda e, j=j, half=half, mc=mc, bank=bank: e.matmul(
                                bank[:],
                                hid[:, mc * MT + j * 128: mc * MT + (j + 1) * 128],
                                wds[:, mc * D + half * 512: mc * D + (half + 1) * 512],
                                start=(mc == 0), stop=(mc == MC - 1)),
                                waits=w, inc="T" if mc == MC - 1 else None)
                        t_d = cnt["T"]
                        emit("V", lambda e, j=j, half=half, bank=bank, e_i=e_i:
                             e.scalar_tensor_tensor(
                                 acc[:, j * D + half * 512: j * D + (half + 1) * 512],
                                 bank[:], gat[:, j * E + e_i: j * E + e_i + 1],
                                 acc[:, j * D + half * 512: j * D + (half + 1) * 512],
                                 op0=ALU.mult, op1=ALU.add),
                             waits=[("T", t_d)], inc="V")
                        t_prev_d[(j * 2 + half) % 2] = cnt["V"]
                        v_evac_d[j] = cnt["V"]
                t_d_done = cnt["T"]
                if not last:
                    nxt_cached = (m > 0) if e_i < E - 1 else True
                    wt_expert = load_expert(ne, t_gu_done, t_d_done, nxt_cached)
                if m < NMACRO - 1:
                    if e_i == 1:
                        ntok0 = (m + 1) * MT
                        for j in range(NCH):
                            emit("G", lambda e, j=j, ntok0=ntok0: e.dma_start(
                                out=x0buf[:, j * D:(j + 1) * D],
                                in_=x_d[ntok0 + j * 128: ntok0 + (j + 1) * 128, :]),
                                waits=[("V", v_x1)], inc="XB", inc_n=16)
                        ld_x_next = cnt["XB"]
                    if e_i in (2, 3):
                        for j in range(4 * (e_i - 2), 4 * (e_i - 1)):
                            emit_stats(x0buf, j, [("XB", ld_x_next)],
                                       [("XB", ld_x_next)])
                        if e_i == 3:
                            front_stats = (cnt["V"], cnt["S"])
                    if e_i == 4:
                        front_chain = ln_chain(front_stats)
                    if e_i == 5:
                        front_marker = ln_normalize(
                            x0buf, front_chain[0], front_chain[1], t_ln2t)
                v_experts_done = cnt["V"]
                v_evac_final = dict(v_evac_d)

            for j in range(NCH):
                emit("SY", lambda e, j=j, tok0=tok0: e.dma_start(
                    out=y_d[tok0 + j * 128: tok0 + (j + 1) * 128, :],
                    in_=acc[:, j * D:(j + 1) * D]),
                    waits=[("V", v_evac_final[j])], inc="ST", inc_n=16)
            st_all_prev = cnt["ST"]

        st_total = cnt["ST"]
        emit("SY", lambda e: e.nop(),
             waits=[("ST", st_total)] + ([("DB", cnt["DB"])] if debug else []))

        block = ctx.enter_context(nc.Block())

        global _SBUF_FREE
        _SBUF_FREE = (nc.sbuf_bytes_remaining, nc.psum_banks_remaining)
        self_sem = {"T": "T", "V": "V", "S": "S"}

        def walker(eng_name):
            def run(eng):
                waited = {}
                skip = self_sem.get(eng_name)
                for rec in steps:
                    if rec["eng"] != eng_name:
                        continue
                    for sname, val in rec["waits"]:
                        if sname == skip:
                            continue
                        if waited.get(sname, 0) >= val:
                            continue
                        eng.wait_ge(sems[sname], val)
                        waited[sname] = val
                    for sname, val in rec.get("fwaits", ()):
                        if waited.get(sname, 0) >= val:
                            continue
                        eng.wait_ge(sems[sname], val)
                        waited[sname] = val
                    inst = rec["fn"](eng)
                    if rec["inc"] is not None:
                        inst.then_inc(sems[rec["inc"]], rec["inc_n"])
            return run

        block.gpsimd(walker("G"))
        block.vector(walker("V"))
        block.scalar(walker("S"))
        block.tensor(walker("T"))
        block.sync(walker("SY"))

    return nc


_NC_CACHE = {}


def kernel(**inputs):
    x = np.ascontiguousarray(
        np.asarray(inputs["L0_attn_x"], dtype=np.float32).reshape(TOK, D))
    wkt = np.ascontiguousarray(np.asarray(inputs["L0_attn_W_K"], dtype=np.float32).T)
    w1t = np.ascontiguousarray(np.asarray(inputs["L0_attn_W1"], dtype=np.float32).T)
    w2t = np.ascontiguousarray(np.asarray(inputs["L0_attn_W2"], dtype=np.float32).T)
    wg8 = np.ascontiguousarray(np.asarray(inputs["L0_ffn_w_gate"], dtype=np.float32))
    wg = np.ascontiguousarray(np.asarray(inputs["L0_ffn_w_g"], dtype=np.float32))
    wu = np.ascontiguousarray(np.asarray(inputs["L0_ffn_w_u"], dtype=np.float32))
    wd = np.ascontiguousarray(np.asarray(inputs["L0_ffn_w_d"], dtype=np.float32))

    if "nc" not in _NC_CACHE:
        _NC_CACHE["nc"] = build_program()
    nc = _NC_CACHE["nc"]

    in_maps = []
    for c in range(NCORES):
        in_maps.append({
            "x": x[c * TPC:(c + 1) * TPC],
            "wkt": wkt, "w1t": w1t, "w2t": w2t, "wgate": wg8,
            "wg": wg, "wu": wu, "wd": wd,
        })
    res = run_bass_kernel_spmd(nc, in_maps, core_ids=list(range(NCORES)))
    y = np.concatenate([r["y"] for r in res.results], axis=0)
    return y.reshape(B, S, D)


# version 24
# speedup vs baseline: 1.0944x; 1.0011x over previous
import numpy as np
from contextlib import ExitStack

import concourse.bass as bass
import concourse.mybir as mybir
from concourse.bass_utils import run_bass_kernel_spmd

F32 = mybir.dt.float32
BF16 = mybir.dt.bfloat16
AF = mybir.ActivationFunctionType
ALU = mybir.AluOpType
AX = mybir.AxisListType

NCORES = 8
B, S, D = 16, 2048, 1024
TOK = B * S
TPC = TOK // NCORES
MT = 1024
NMACRO = TPC // MT
NCH = MT // 128
DC = D // 128
M = 1408
MC = M // 128
E = 8
HD = 64


def build_program(debug=False):
    nc = bass.Bass("TRN2", target_bir_lowering=False,
                   detect_race_conditions=False)
    x_d = nc.dram_tensor("x", [TPC, D], F32, kind="ExternalInput").ap()
    wkt_d = nc.dram_tensor("wkt", [D, D], F32, kind="ExternalInput").ap()
    w1t_d = nc.dram_tensor("w1t", [D, HD], F32, kind="ExternalInput").ap()
    w2t_d = nc.dram_tensor("w2t", [HD, D], F32, kind="ExternalInput").ap()
    wg8_d = nc.dram_tensor("wgate", [D, E], F32, kind="ExternalInput").ap()
    wg_d = nc.dram_tensor("wg", [E, D, M], F32, kind="ExternalInput").ap()
    wu_d = nc.dram_tensor("wu", [E, D, M], F32, kind="ExternalInput").ap()
    wd_d = nc.dram_tensor("wd", [E, M, D], F32, kind="ExternalInput").ap()
    y_d = nc.dram_tensor("y", [TPC, D], F32, kind="ExternalOutput").ap()
    dbg_d = {}
    if debug:
        for nm, shape in [("d_x1", [128, NCH * D]), ("d_lnf", [128, DC * MT]),
                          ("d_kfm", [128, DC * 512]), ("d_a", [128, MT]),
                          ("d_scs", [128, NCH * E]), ("d_gat", [128, NCH * E]),
                          ("d_lnf2", [128, DC * MT]), ("d_ltm", [128, NCH * D]), ("d_x0b", [128, NCH * D]), ("d_st", [128, 48])]:
            dbg_d[nm] = nc.dram_tensor(nm, shape, F32, kind="ExternalOutput").ap()

    import ml_dtypes
    ident_d = nc.inline_tensor(np.eye(128, dtype=ml_dtypes.bfloat16), name="ident")
    wgbf_d = nc.dram_tensor("wgbf", [E, 128, DC * M], BF16, kind="Internal").ap()
    wubf_d = nc.dram_tensor("wubf", [E, 128, DC * M], BF16, kind="Internal").ap()
    wdbf_d = nc.dram_tensor("wdbf", [E, 128, MC * D], BF16, kind="Internal").ap()

    with ExitStack() as ctx:
        SB = lambda name, shape, dt: ctx.enter_context(nc.sbuf_tensor(name, shape, dt))
        PS = lambda name, shape, dt: ctx.enter_context(nc.psum_tensor(name, shape, dt))

        acc = SB("acc", [128, NCH * D], F32)
        x0buf = SB("x0buf", [128, NCH * D], BF16)
        ln_tm = SB("ln_tm", [128, NCH * D], BF16)
        lnf = SB("lnf", [128, DC * MT], BF16)
        kfm = SB("kfm", [128, DC * 512], BF16)
        a_sb = SB("a_sb", [128, MT], BF16)
        hid = SB("hid", [128, MC * MT], BF16)
        wgs = SB("wgs", [128, DC * M], BF16)
        wus = SB("wus", [128, DC * M], BF16)
        wds = SB("wds", [128, MC * D], BF16)
        wkt = SB("wkt_sb", [128, DC * D], BF16)
        w1t = SB("w1t_sb", [128, DC * HD], BF16)
        w2t = SB("w2t_sb", [128, D], BF16)
        wg8 = SB("wg8_sb", [128, DC * E], BF16)
        ident = SB("ident_sb", [128, 128], BF16)
        sqscr = SB("sqscr", [128, D], F32)
        silu0 = SB("silu0", [128, 512], BF16)
        silu1 = SB("silu1", [128, 512], BF16)
        sum8 = SB("sum8", [128, NCH], F32)
        sq8 = SB("sq8", [128, NCH], F32)
        mean8 = SB("mean8", [128, NCH], F32)
        exsq8 = SB("exsq8", [128, NCH], F32)
        msq8 = SB("msq8", [128, NCH], F32)
        var8 = SB("var8", [128, NCH], F32)
        lnv8 = SB("lnv8", [128, NCH], F32)
        inv8 = SB("inv8", [128, NCH], F32)
        scs = SB("scs", [128, NCH * E], F32)
        mx8 = SB("mx8", [128, NCH * E], F32)
        nm18 = SB("nm18", [128, NCH], F32)
        exall = SB("exall", [128, NCH * E], F32)
        den8 = SB("den8", [128, NCH], F32)
        den8p = SB("den8p", [128, NCH], F32)
        rden8 = SB("rden8", [128, NCH], F32)
        mskall = SB("mskall", [128, NCH * E], F32)
        gat = SB("gat", [128, NCH * E], F32)
        nmi8 = SB("nmi8", [128, NCH], F32)
        bias2 = SB("bias2", [128, NCH], F32)
        invD = SB("invD", [128, 1], F32)
        epsc = SB("epsc", [128, 1], F32)
        nhc = SB("nhc", [128, 1], F32)
        onec = SB("onec", [128, 1], F32)
        negone = SB("negone", [128, 1], F32)

        pbt = PS("pbt", [128, 1024], BF16)
        pk = [PS("pk0", [128, 512], F32), PS("pk1", [128, 512], F32)]
        pg = [PS("pg0", [128, 512], F32), PS("pg1", [128, 512], F32)]
        pu = [PS("pu0", [128, 512], F32), PS("pu1", [128, 512], F32)]
        pbt2 = PS("pbt2", [128, 1024], BF16)

        sems = {}
        for name in ("LD", "XB", "WA", "WG", "WU", "WD", "T", "V", "S", "ST", "DB"):
            sems[name] = ctx.enter_context(nc.semaphore(name))

        steps = []
        cnt = {k: 0 for k in sems}

        def emit(eng, fn, waits=(), inc=None, inc_n=1, fwaits=()):
            steps.append({"eng": eng, "waits": [w for w in waits if w[1] > 0],
                          "fwaits": [w for w in fwaits if w[1] > 0],
                          "fn": fn, "inc": inc, "inc_n": inc_n})
            if inc is not None:
                cnt[inc] += inc_n
            return cnt[inc] if inc is not None else None

        emit("V", lambda e: e.memset(invD[:], 1.0 / D))
        emit("V", lambda e: e.memset(epsc[:], 1e-5))
        emit("V", lambda e: e.memset(nhc[:], -0.5))
        emit("V", lambda e: e.memset(negone[:], -1.0))
        emit("V", lambda e: e.memset(onec[:], 1.0), inc="V")
        v_consts = cnt["V"]

        emit("SY", lambda e: e.dma_start(out=ident[:], in_=ident_d.ap()), inc="LD", inc_n=16)
        ld_ident = cnt["LD"]

        st_all_prev = 0
        ld_x_next = []
        for j in range(NCH):
            emit("G", lambda e, j=j: e.dma_start(
                out=x0buf[:, j * D:(j + 1) * D],
                in_=x_d[j * 128:(j + 1) * 128, :]), inc="XB", inc_n=16)
        ld_x_next = cnt["XB"]

        for hc in range(DC):
            emit("G", lambda e, hc=hc: e.dma_start(
                out=wkt[:, hc * D:(hc + 1) * D], in_=wkt_d[hc * 128:(hc + 1) * 128, :]),
                inc="WA", inc_n=16)
        for kc in range(DC):
            emit("G", lambda e, kc=kc: e.dma_start(
                out=w1t[:, kc * HD:(kc + 1) * HD], in_=w1t_d[kc * 128:(kc + 1) * 128, :]),
                inc="WA", inc_n=16)
        emit("G", lambda e: e.dma_start(out=w2t[0:HD, :], in_=w2t_d), inc="WA", inc_n=16)
        for dc in range(DC):
            emit("G", lambda e, dc=dc: e.dma_start(
                out=wg8[:, dc * E:(dc + 1) * E], in_=wg8_d[dc * 128:(dc + 1) * 128, :]),
                inc="WA", inc_n=16)
        wt_attn = cnt["WA"]

        wt_wrote = {"g": {}, "u": {}, "d": {}}

        def load_expert(e_idx, wait_t_gu, wait_t_d, cached):
            if not cached:
                wg0, wu0, wd0 = cnt["WG"], cnt["WU"], cnt["WD"]
                for dc in range(DC):
                    emit("G", lambda e, dc=dc, ei=e_idx: e.dma_start(
                        out=wgs[:, dc * M:(dc + 1) * M],
                        in_=wg_d[ei, dc * 128:(dc + 1) * 128, :]),
                        waits=[("T", wait_t_gu), ("WG", wg0)], inc="WG", inc_n=16)
                wt_lg = cnt["WG"]
                for dc in range(DC):
                    emit("G", lambda e, dc=dc, ei=e_idx: e.dma_start(
                        out=wus[:, dc * M:(dc + 1) * M],
                        in_=wu_d[ei, dc * 128:(dc + 1) * 128, :]),
                        waits=[("T", wait_t_gu), ("WU", wu0)], inc="WU", inc_n=16)
                wt_lu = cnt["WU"]
                emit("G", lambda e, ei=e_idx: e.dma_start(
                    out=wgbf_d[ei], in_=wgs[:]),
                    waits=[("WG", wt_lg)], inc="WG", inc_n=16)
                wt_wrote["g"][e_idx] = cnt["WG"]
                emit("G", lambda e, ei=e_idx: e.dma_start(
                    out=wubf_d[ei], in_=wus[:]),
                    waits=[("WU", wt_lu)], inc="WU", inc_n=16)
                wt_wrote["u"][e_idx] = cnt["WU"]
                gu_waits = [("WG", wt_lg), ("WU", wt_lu)]
                for mc in range(MC):
                    emit("G", lambda e, mc=mc, ei=e_idx: e.dma_start(
                        out=wds[:, mc * D:(mc + 1) * D],
                        in_=wd_d[ei, mc * 128:(mc + 1) * 128, :]),
                        waits=[("T", wait_t_d), ("WD", wd0)], inc="WD", inc_n=16)
                wt_ld = cnt["WD"]
                emit("G", lambda e, ei=e_idx: e.dma_start(
                    out=wdbf_d[ei], in_=wds[:]),
                    waits=[("WD", wt_ld)], inc="WD", inc_n=16)
                wt_wrote["d"][e_idx] = cnt["WD"]
                d_waits = [("WD", wt_ld)]
            else:
                emit("G", lambda e, ei=e_idx: e.dma_start(
                    out=wgs[:], in_=wgbf_d[ei]),
                    waits=[("T", wait_t_gu), ("WG", cnt["WG"])],
                    inc="WG", inc_n=16)
                wg_c = cnt["WG"]
                emit("G", lambda e, ei=e_idx: e.dma_start(
                    out=wus[:], in_=wubf_d[ei]),
                    waits=[("T", wait_t_gu), ("WU", cnt["WU"])],
                    inc="WU", inc_n=16)
                gu_waits = [("WG", wg_c), ("WU", cnt["WU"])]
                emit("G", lambda e, ei=e_idx: e.dma_start(
                    out=wds[:], in_=wdbf_d[ei]),
                    waits=[("T", wait_t_d), ("WD", cnt["WD"])],
                    inc="WD", inc_n=16)
                d_waits = [("WD", cnt["WD"])]
            return gu_waits, d_waits

        wt_expert = load_expert(0, 0, 0, cached=False)

        NLEXP_SET, GELU_SET, SILU_SET = 6, 10, 18

        def preload_set(sid):
            pass

        v_ln1t_done_prev = 0

        sqscr_last = [0]

        def emit_stats(src, j, w_s, w_v):
            emit("S", lambda e, j=j: e.activation(
                sqscr[:], src[:, j * D:(j + 1) * D], AF.Square,
                accum_out=sq8[:, j:j + 1]), waits=w_s,
                fwaits=[("S", sqscr_last[0])], inc="S")
            sqscr_last[0] = cnt["S"]
            s_m = cnt["S"]
            emit("V", lambda e, j=j: e.reduce_sum(
                sum8[:, j:j + 1], src[:, j * D:(j + 1) * D], axis=AX.X),
                waits=w_v + [("S", s_m)], inc="V")

        def ln_chain(stats_done, c0=0, c1=NCH):
            v_sums, s_sqs = stats_done
            cs = slice(c0, c1)
            emit("S", lambda e: e.activation(mean8[:, cs], sum8[:, cs],
                                             AF.Identity, scale=invD[:, 0:1]),
                 waits=[("V", v_sums)], inc="S")
            emit("S", lambda e: e.activation(exsq8[:, cs], sq8[:, cs],
                                             AF.Identity, scale=invD[:, 0:1]),
                 inc="S")
            emit("S", lambda e: e.activation(msq8[:, cs], mean8[:, cs],
                                             AF.Square), inc="S")
            s_stats = cnt["S"]
            emit("V", lambda e: e.tensor_sub(var8[:, cs], exsq8[:, cs],
                                             msq8[:, cs]),
                 waits=[("S", s_stats)], inc="V")
            v_var = cnt["V"]
            emit("S", lambda e: e.activation(lnv8[:, cs], var8[:, cs], AF.Ln,
                                             bias=epsc[:, 0:1]),
                 waits=[("V", v_var)], inc="S")
            emit("S", lambda e: e.activation(inv8[:, cs], lnv8[:, cs], AF.Exp,
                                             scale=nhc[:, 0:1]), inc="S")
            s_inv = cnt["S"]
            emit("V", lambda e: e.tensor_mul(nmi8[:, cs], mean8[:, cs],
                                             inv8[:, cs]),
                 waits=[("S", s_inv)])
            emit("V", lambda e: e.tensor_scalar_mul(bias2[:, cs], nmi8[:, cs],
                                                    negone[:, 0:1]), inc="V")
            return s_inv, cnt["V"]

        def ln_normalize(src, s_inv, v_b2, t_lntm_free, chunks=range(NCH)):
            norm_marker = {}
            for j in chunks:
                if j % 2 == 0 or j == 7:
                    emit("V", lambda e, j=j: e.tensor_scalar(
                        ln_tm[:, j * D:(j + 1) * D], src[:, j * D:(j + 1) * D],
                        mean8[:, j:j + 1], inv8[:, j:j + 1],
                        op0=ALU.subtract, op1=ALU.mult),
                        waits=[("S", s_inv), ("T", t_lntm_free)], inc="V")
                    norm_marker[j] = ("V", cnt["V"])
                else:
                    emit("S", lambda e, j=j: e.activation(
                        ln_tm[:, j * D:(j + 1) * D], src[:, j * D:(j + 1) * D],
                        AF.Identity, scale=inv8[:, j:j + 1],
                        bias=bias2[:, j:j + 1]),
                        waits=[("V", v_b2), ("T", t_lntm_free)], inc="S")
                    norm_marker[j] = ("S", cnt["S"])
            return norm_marker

        def ln_chain_norm(src, stats_done, t_lntm_free):
            s_inv, v_b2 = ln_chain(stats_done)
            return ln_normalize(src, s_inv, v_b2, t_lntm_free)

        pbt_state = {}

        def ln_transpose(norm_marker, chunks=range(NCH)):
            lnf_v = lnf[:].rearrange("p (c t) -> p c t", t=MT)
            banks = (pbt, pbt2)
            v_evac_last = s_evac_last = None
            for j in chunks:
                bank = banks[j % 2]
                bank_v = bank[:].rearrange("p (c t) -> p c t", t=128)
                for c in range(DC):
                    w = []
                    if c == 0:
                        w = [norm_marker[j], ("LD", ld_ident)]
                        prev = pbt_state.get(j % 2)
                        if prev:
                            w.append(prev)
                    emit("T", lambda e, c=c, j=j, bank=bank: e.transpose(
                        bank[:, c * 128:(c + 1) * 128],
                        ln_tm[:, j * D + c * 128: j * D + (c + 1) * 128],
                        ident[:]),
                        waits=w, inc="T" if c == DC - 1 else None)
                t_j = cnt["T"]
                emit("V", lambda e, j=j, bank_v=bank_v: e.tensor_copy(
                    lnf_v[:, :, j * 128:(j + 1) * 128], bank_v[:]),
                    waits=[("T", t_j)], inc="V")
                pbt_state[j % 2] = ("V", cnt["V"])
                v_evac_last = cnt["V"]
                s_evac_last = cnt["S"]
            return v_evac_last, s_evac_last, cnt["T"]

        front_marker = None

        for m in range(NMACRO):
            tok0 = m * MT
            ld_x = ld_x_next

            if front_marker is None:
                preload_set(NLEXP_SET)
                for j in range(NCH):
                    emit_stats(x0buf, j, [("XB", ld_x)], [("XB", ld_x)])
                front_marker = ln_chain_norm(x0buf, (cnt["V"], cnt["S"]),
                                             v_ln1t_done_prev)
            v_lnf1, s_lnf1, t_ln1t = ln_transpose(front_marker)
            front_marker = None
            if debug and m == 1:
                emit("G", lambda e: e.dma_start(out=dbg_d["d_lnf"], in_=lnf[:]),
                     waits=[("V", v_lnf1), ("S", s_lnf1)], inc="DB", inc_n=16)
                emit("G", lambda e: e.dma_start(out=dbg_d["d_ltm"], in_=ln_tm[:]),
                     waits=[("V", v_lnf1), ("S", s_lnf1)], inc="DB", inc_n=16)
                emit("G", lambda e: e.dma_start(out=dbg_d["d_x0b"], in_=x0buf[:]),
                     waits=[("V", v_lnf1), ("S", s_lnf1)], inc="DB", inc_n=16)
                for k, tile in enumerate((sum8, sq8, mean8, var8, inv8, bias2)):
                    emit("G", lambda e, k=k, tile=tile: e.dma_start(
                        out=dbg_d["d_st"][:, k * 8:(k + 1) * 8], in_=tile[:]),
                        waits=[("V", v_lnf1), ("S", s_lnf1)], inc="DB", inc_n=16)
            preload_set(GELU_SET)

            pk_free = {0: ("V", cnt["V"]), 1: ("V", cnt["V"])}
            v_kdone = cnt["V"]
            s_kdone = cnt["S"]

            stats_pending = []

            def flush_stats(k):
                while stats_pending and k > 0:
                    j, ve = stats_pending.pop(0)
                    emit("S", lambda e, j=j: e.activation(
                        sqscr[:], acc[:, j * D:(j + 1) * D], AF.Square,
                        accum_out=sq8[:, j:j + 1]), waits=[("V", ve)],
                        fwaits=[("S", sqscr_last[0])], inc="S")
                    sqscr_last[0] = cnt["S"]
                    emit("S", lambda e, j=j: e.activation(
                        sqscr[:], acc[:, j * D:(j + 1) * D], AF.Identity,
                        accum_out=sum8[:, j:j + 1]),
                        fwaits=[("S", sqscr_last[0])], inc="S")
                    sqscr_last[0] = cnt["S"]
                    k -= 1

            rbanks = [pk[0], pk[1], pg[0], pg[1], pu[1], pu[0]]
            rfree = {0: pk_free[0], 1: pk_free[1], 2: None, 3: None,
                     4: None, 5: None}

            def kproj(s):
                for kc in range(DC):
                    bi = kc % 2
                    bank = pk[bi]
                    for hc in range(DC):
                        emit("T", lambda e, s=s, kc=kc, hc=hc, bank=bank: e.matmul(
                            bank[:],
                            wkt[:, hc * D + kc * 128: hc * D + (kc + 1) * 128],
                            lnf[:, hc * MT + s * 512: hc * MT + (s + 1) * 512],
                            start=(hc == 0), stop=(hc == DC - 1)),
                            waits=([("V", v_lnf1), ("S", s_lnf1), ("WA", wt_attn)]
                                   + ([rfree[bi]] if rfree[bi] else [])
                                   if hc == 0 else []),
                            inc="T" if hc == DC - 1 else None)
                    t_k = cnt["T"]
                    eng = "V" if kc % 2 == 0 else "S"
                    if eng == "V":
                        emit("V", lambda e, kc=kc, bank=bank: e.tensor_copy(
                            kfm[:, kc * 512:(kc + 1) * 512],
                            bank[:]), waits=[("T", t_k)], inc="V")
                    else:
                        emit("S", lambda e, kc=kc, bank=bank: e.copy(
                            kfm[:, kc * 512:(kc + 1) * 512],
                            bank[:]), waits=[("T", t_k)], inc="S")
                    rfree[bi] = (eng, cnt[eng])
                return cnt["V"], cnt["S"]

            def hpass(s, vk, sk):
                for kc in range(DC):
                    emit("T", lambda e, s=s, kc=kc: e.matmul(
                        pu[0][0:HD, :],
                        w1t[:, kc * HD:(kc + 1) * HD],
                        kfm[:, kc * 512:(kc + 1) * 512],
                        start=(kc == 0), stop=(kc == DC - 1)),
                        waits=([("V", vk), ("S", sk)] if kc == 0 else []),
                        inc="T" if kc == DC - 1 else None)
                t_h = cnt["T"]
                emit("S", lambda e, s=s: e.activation(
                    a_sb[0:HD, s * 512:(s + 1) * 512], pu[0][0:HD, :], AF.Gelu),
                    waits=[("T", t_h)], inc="S")
                return cnt["S"]

            def reads(s, s_gelu):
                bset = (0, 1, 2, 3, 4) if s == 0 else (1, 2, 3, 4, 5)
                for jj in range(4):
                    j = s * 4 + jj
                    for half in range(2):
                        bi = bset[(jj * 2 + half) % 5]
                        bank = rbanks[bi]
                        emit("T", lambda e, s=s, jj=jj, half=half, bank=bank: e.matmul(
                            bank[:],
                            a_sb[0:HD, s * 512 + jj * 128: s * 512 + (jj + 1) * 128],
                            w2t[0:HD, half * 512:(half + 1) * 512],
                            start=True, stop=True),
                            waits=[("S", s_gelu)]
                            + ([rfree[bi]] if rfree[bi] else []), inc="T")
                        t_r = cnt["T"]
                        emit("V", lambda e, j=j, half=half, bank=bank: e.tensor_add(
                            acc[:, j * D + half * 512: j * D + (half + 1) * 512],
                            x0buf[:, j * D + half * 512: j * D + (half + 1) * 512],
                            bank[:]),
                            waits=[("T", t_r), ("ST", st_all_prev)], inc="V")
                        rfree[bi] = ("V", cnt["V"])
                    stats_pending.append((j, cnt["V"]))

            vk0, sk0 = kproj(0)
            sg0 = hpass(0, vk0, sk0)
            vk1, sk1 = kproj(1)
            reads(0, sg0)
            sg1 = hpass(1, vk1, sk1)
            preload_set(NLEXP_SET)
            flush_stats(4)
            ch_a = ln_chain((cnt["V"], cnt["S"]), 0, 4)
            nm_a = ln_normalize(acc, ch_a[0], ch_a[1], t_ln1t, chunks=range(4))
            reads(1, sg1)
            va, sa, _ = ln_transpose(nm_a, chunks=range(4))
            flush_stats(8)
            v_x1 = cnt["V"]
            if debug and m == 0:
                emit("SY", lambda e: e.dma_start(out=dbg_d["d_x1"], in_=acc[:]),
                     waits=[("V", v_x1)], inc="DB", inc_n=16)

            ch_b = ln_chain((cnt["V"], cnt["S"]), 4, NCH)
            nm_b = ln_normalize(acc, ch_b[0], ch_b[1], t_ln1t,
                                chunks=range(4, NCH))
            v_lnf2, s_lnf2, t_ln2t = ln_transpose(nm_b, chunks=range(4, NCH))
            v_lnf2 = max(v_lnf2, va)
            s_lnf2 = max(s_lnf2 or 0, sa or 0)
            if debug and m == 0:
                emit("G", lambda e: e.dma_start(out=dbg_d["d_lnf2"], in_=lnf[:]),
                     waits=[("V", v_lnf2)], inc="DB", inc_n=16)
                emit("G", lambda e: e.dma_start(out=dbg_d["d_a"], in_=a_sb[:]),
                     waits=[("V", v_lnf2)], inc="DB", inc_n=16)
            v_ln1t_done_prev = t_ln2t

            for j in range(NCH):
                for dc in range(DC):
                    emit("T", lambda e, j=j, dc=dc: e.matmul(
                        pu[1][:, j * E:(j + 1) * E],
                        lnf[:, dc * MT + j * 128: dc * MT + (j + 1) * 128],
                        wg8[:, dc * E:(dc + 1) * E],
                        start=(dc == 0), stop=(dc == DC - 1)),
                        waits=([("V", v_lnf2), ("S", s_lnf2)]
                               if dc == 0 and j == 0 else []),
                        inc="T" if dc == DC - 1 else None)
            t_scores = cnt["T"]
            emit("S", lambda e: e.copy(scs[:], pu[1][:, 0:NCH * E]),
                 waits=[("T", t_scores)], inc="S")
            s_scs = cnt["S"]
            for j in range(NCH):
                emit("V", lambda e, j=j: e.max(out=mx8[:, j * E:(j + 1) * E],
                                               in_=scs[:, j * E:(j + 1) * E]),
                     waits=[("S", s_scs)])
            for j in range(NCH):
                emit("V", lambda e, j=j: e.reduce_max(
                    nm18[:, j:j + 1], scs[:, j * E:(j + 1) * E], axis=AX.X,
                    negate=True), inc=("V" if j == NCH - 1 else None))
            v_mx = cnt["V"]
            for j in range(NCH):
                emit("S", lambda e, j=j: e.activation(
                    exall[:, j * E:(j + 1) * E], scs[:, j * E:(j + 1) * E], AF.Exp,
                    bias=nm18[:, j:j + 1]), waits=[("V", v_mx)] if j == 0 else [])
            for j in range(NCH):
                emit("S", lambda e, j=j: e.activation(
                    den8[:, j:j + 1], mx8[:, j * E + 1: j * E + 2], AF.Exp,
                    bias=nm18[:, j:j + 1]))
            emit("S", lambda e: e.activation(den8p[:], den8[:], AF.Identity,
                                             bias=onec[:, 0:1]), inc="S")
            s_ex = cnt["S"]
            emit("V", lambda e: e.reciprocal(rden8[:], den8p[:]),
                 waits=[("S", s_ex)])
            for j in range(NCH):
                emit("V", lambda e, j=j: e.tensor_scalar(
                    mskall[:, j * E:(j + 1) * E], scs[:, j * E:(j + 1) * E],
                    mx8[:, j * E + 1: j * E + 2], None, op0=ALU.is_ge),
                    inc=("V" if j == NCH - 1 else None))
            v_msk = cnt["V"]
            emit("S", lambda e: e.nop(), waits=[("V", v_msk)], inc="S")
            s_ping = cnt["S"]
            for j in range(NCH):
                emit("V", lambda e, j=j: e.scalar_tensor_tensor(
                    gat[:, j * E:(j + 1) * E], exall[:, j * E:(j + 1) * E],
                    rden8[:, j:j + 1], mskall[:, j * E:(j + 1) * E],
                    op0=ALU.mult, op1=ALU.mult),
                    waits=[("S", s_ping)] if j == 0 else [],
                    inc=("V" if j == NCH - 1 else None))
            v_gat = cnt["V"]
            preload_set(SILU_SET)
            if debug and m == 0:
                emit("SY", lambda e: e.dma_start(out=dbg_d["d_scs"], in_=scs[:]),
                     waits=[("V", v_gat)], inc="DB", inc_n=16)
                emit("SY", lambda e: e.dma_start(out=dbg_d["d_gat"], in_=gat[:]),
                     waits=[("V", v_gat)], inc="DB", inc_n=16)

            s_silu = {}
            v_hid = {}
            t_gu_prev_bank = {}
            for e_i in range(E):
                gu_waits_now, d_waits_now = wt_expert
                v_hid_mc = {}
                first_mm = True
                for mc in range(MC):
                    for s in range(2):
                        for dc in range(DC):
                            w = []
                            if first_mm:
                                w = list(gu_waits_now)
                                if e_i == 0:
                                    w.append(("S", s_scs))
                                first_mm = False
                            if dc == 0:
                                prev = t_gu_prev_bank.get(("g", s))
                                if prev is not None:
                                    w.append(("S", prev))
                            emit("T", lambda e, mc=mc, s=s, dc=dc: e.matmul(
                                pg[s][:],
                                wgs[:, dc * M + mc * 128: dc * M + (mc + 1) * 128],
                                lnf[:, dc * MT + s * 512: dc * MT + (s + 1) * 512],
                                start=(dc == 0), stop=(dc == DC - 1)),
                                waits=w, inc="T" if dc == DC - 1 else None)
                        t_g = cnt["T"]
                        for dc in range(DC):
                            w = []
                            if dc == 0:
                                prev = t_gu_prev_bank.get(("u", s))
                                if prev is not None:
                                    w.append(("V", prev))
                            emit("T", lambda e, mc=mc, s=s, dc=dc: e.matmul(
                                pu[s][:],
                                wus[:, dc * M + mc * 128: dc * M + (mc + 1) * 128],
                                lnf[:, dc * MT + s * 512: dc * MT + (s + 1) * 512],
                                start=(dc == 0), stop=(dc == DC - 1)),
                                waits=w, inc="T" if dc == DC - 1 else None)
                        t_u = cnt["T"]
                        sbuf_s = silu0 if s == 0 else silu1
                        silu_war = t_gu_prev_bank.get(("u", s))
                        emit("S", lambda e, sb=sbuf_s, s=s: e.activation(
                            sb[:], pg[s][:], AF.Silu),
                            waits=[("T", t_g)]
                            + ([("V", silu_war)] if silu_war is not None else []),
                            inc="S")
                        t_gu_prev_bank[("g", s)] = cnt["S"]
                        s_act = cnt["S"]
                        emit("V", lambda e, sb=sbuf_s, s=s, mc=mc: e.tensor_mul(
                            hid[:, mc * MT + s * 512: mc * MT + (s + 1) * 512],
                            sb[:], pu[s][:]),
                            waits=[("S", s_act), ("T", t_u)], inc="V")
                        t_gu_prev_bank[("u", s)] = cnt["V"]
                        if s == 1:
                            v_hid_mc[mc] = cnt["V"]
                v_hidden = cnt["V"]
                t_gu_done = cnt["T"]
                ne = (e_i + 1) % E
                last = (m == NMACRO - 1 and e_i == E - 1)
                v_evac_d = {}
                t_prev_d = {}
                for j in range(NCH):
                    for half in range(2):
                        bank = pk[(j * 2 + half) % 2]
                        for mc in range(MC):
                            w = [("V", v_hid_mc[mc])]
                            if mc == 0:
                                if j == 0 and half == 0:
                                    w.extend(d_waits_now)
                                    w.append(("V", v_gat))
                                prev = t_prev_d.get((j * 2 + half) % 2)
                                if prev is not None:
                                    w.append(("V", prev))
                            emit("T", lambda e, j=j, half=half, mc=mc, bank=bank: e.matmul(
                                bank[:],
                                hid[:, mc * MT + j * 128: mc * MT + (j + 1) * 128],
                                wds[:, mc * D + half * 512: mc * D + (half + 1) * 512],
                                start=(mc == 0), stop=(mc == MC - 1)),
                                waits=w, inc="T" if mc == MC - 1 else None)
                        t_d = cnt["T"]
                        emit("V", lambda e, j=j, half=half, bank=bank, e_i=e_i:
                             e.scalar_tensor_tensor(
                                 acc[:, j * D + half * 512: j * D + (half + 1) * 512],
                                 bank[:], gat[:, j * E + e_i: j * E + e_i + 1],
                                 acc[:, j * D + half * 512: j * D + (half + 1) * 512],
                                 op0=ALU.mult, op1=ALU.add),
                             waits=[("T", t_d)], inc="V")
                        t_prev_d[(j * 2 + half) % 2] = cnt["V"]
                        v_evac_d[j] = cnt["V"]
                t_d_done = cnt["T"]
                if not last:
                    nxt_cached = (m > 0) if e_i < E - 1 else True
                    wt_expert = load_expert(ne, t_gu_done, t_d_done, nxt_cached)
                if m < NMACRO - 1:
                    if e_i == 1:
                        ntok0 = (m + 1) * MT
                        for j in range(NCH):
                            emit("G", lambda e, j=j, ntok0=ntok0: e.dma_start(
                                out=x0buf[:, j * D:(j + 1) * D],
                                in_=x_d[ntok0 + j * 128: ntok0 + (j + 1) * 128, :]),
                                waits=[("V", v_x1)], inc="XB", inc_n=16)
                        ld_x_next = cnt["XB"]
                    if e_i in (2, 3):
                        for j in range(4 * (e_i - 2), 4 * (e_i - 1)):
                            emit_stats(x0buf, j, [("XB", ld_x_next)],
                                       [("XB", ld_x_next)])
                        if e_i == 3:
                            front_stats = (cnt["V"], cnt["S"])
                    if e_i == 4:
                        front_chain = ln_chain(front_stats)
                    if e_i == 5:
                        front_marker = ln_normalize(
                            x0buf, front_chain[0], front_chain[1], t_ln2t)
                v_experts_done = cnt["V"]
                v_evac_final = dict(v_evac_d)

            for j in range(NCH):
                emit("SY", lambda e, j=j, tok0=tok0: e.dma_start(
                    out=y_d[tok0 + j * 128: tok0 + (j + 1) * 128, :],
                    in_=acc[:, j * D:(j + 1) * D]),
                    waits=[("V", v_evac_final[j])], inc="ST", inc_n=16)
            st_all_prev = cnt["ST"]

        st_total = cnt["ST"]
        emit("SY", lambda e: e.nop(),
             waits=[("ST", st_total)] + ([("DB", cnt["DB"])] if debug else []))

        block = ctx.enter_context(nc.Block())

        global _SBUF_FREE
        _SBUF_FREE = (nc.sbuf_bytes_remaining, nc.psum_banks_remaining)
        self_sem = {"T": "T", "V": "V", "S": "S"}

        def walker(eng_name):
            def run(eng):
                waited = {}
                skip = self_sem.get(eng_name)
                for rec in steps:
                    if rec["eng"] != eng_name:
                        continue
                    for sname, val in rec["waits"]:
                        if sname == skip:
                            continue
                        if waited.get(sname, 0) >= val:
                            continue
                        eng.wait_ge(sems[sname], val)
                        waited[sname] = val
                    for sname, val in rec.get("fwaits", ()):
                        if waited.get(sname, 0) >= val:
                            continue
                        eng.wait_ge(sems[sname], val)
                        waited[sname] = val
                    inst = rec["fn"](eng)
                    if rec["inc"] is not None:
                        inst.then_inc(sems[rec["inc"]], rec["inc_n"])
            return run

        block.gpsimd(walker("G"))
        block.vector(walker("V"))
        block.scalar(walker("S"))
        block.tensor(walker("T"))
        block.sync(walker("SY"))

    return nc


_NC_CACHE = {}


def kernel(**inputs):
    x = np.ascontiguousarray(
        np.asarray(inputs["L0_attn_x"], dtype=np.float32).reshape(TOK, D))
    wkt = np.ascontiguousarray(np.asarray(inputs["L0_attn_W_K"], dtype=np.float32).T)
    w1t = np.ascontiguousarray(np.asarray(inputs["L0_attn_W1"], dtype=np.float32).T)
    w2t = np.ascontiguousarray(np.asarray(inputs["L0_attn_W2"], dtype=np.float32).T)
    wg8 = np.ascontiguousarray(np.asarray(inputs["L0_ffn_w_gate"], dtype=np.float32))
    wg = np.ascontiguousarray(np.asarray(inputs["L0_ffn_w_g"], dtype=np.float32))
    wu = np.ascontiguousarray(np.asarray(inputs["L0_ffn_w_u"], dtype=np.float32))
    wd = np.ascontiguousarray(np.asarray(inputs["L0_ffn_w_d"], dtype=np.float32))

    if "nc" not in _NC_CACHE:
        _NC_CACHE["nc"] = build_program()
    nc = _NC_CACHE["nc"]

    in_maps = []
    for c in range(NCORES):
        in_maps.append({
            "x": x[c * TPC:(c + 1) * TPC],
            "wkt": wkt, "w1t": w1t, "w2t": w2t, "wgate": wg8,
            "wg": wg, "wu": wu, "wd": wd,
        })
    res = run_bass_kernel_spmd(nc, in_maps, core_ids=list(range(NCORES)))
    y = np.concatenate([r["y"] for r in res.results], axis=0)
    return y.reshape(B, S, D)
